# version 01
# baseline (speedup 1.0000x reference)
from contextlib import ExitStack

import numpy as np

import concourse.bass as bass
import concourse.mybir as mybir
from concourse.bass_utils import run_bass_kernel_spmd

F32 = mybir.dt.float32
BF16 = mybir.dt.bfloat16
I32 = mybir.dt.int32
AF = mybir.ActivationFunctionType
ALU = mybir.AluOpType

D = 2048
DC = 16
T = 2048
NB = 2
H = 16
HD = 128
ITER = 5504
NIT = 43
SCALE = 1.0 / float(np.sqrt(HD))
C0 = 12.0
EPS = 1e-5
RD = 1.0 / D
RSD = 1.0 / float(np.sqrt(D))

AR_XFM = 0
AR_QKST = 32768
AR_QH = 34816
AR_KH = 38912
AR_VH = 43008
AR_PBT = 47104
AR_AVST = 49152
AR_RC = 50176
AR_RB = 51200
AR_TRST = 52224
AR_BPR = 54272
AR_H = 0
AR_X1FM = 44032
AR_GB = 60416
AR_COLS = 62464

PF = 3

DMA_SEMS = (
    ["lxa", "lxb", "lw0", "lw1", "lw2", "lw3", "ld0", "ld1", "lp0", "lp1", "lp2",
     "lt0", "lt1", "lxf", "lwp", "lbq", "lbp", "lcs"]
    + ["sq0", "sq1", "sq2", "sq3", "sv0", "sv1", "sm0", "sm1"]
    + [f"sg{i}" for i in range(8)]
    + ["sxf", "sp0", "sp1", "so0", "so1", "sxa", "sxt"]
)
ALL_SEMS = DMA_SEMS + ["tP", "vD", "aA", "gP"]
LX = ["lxa", "lxb"]
LW = ["lw0", "lw1", "lw2", "lw3"]
LD_ = ["ld0", "ld1"]
LP = ["lp0", "lp1", "lp2"]
LT = ["lt0", "lt1"]
SQ = ["sq0", "sq1", "sq2", "sq3"]
SV = ["sv0", "sv1"]
SM = ["sm0", "sm1"]
SG = [f"sg{i}" for i in range(8)]
SP_ = ["sp0", "sp1"]
SO = ["so0", "so1"]


class _Instr:
    __slots__ = ("p", "real")

    def __init__(self, p, real):
        self.p = p
        self.real = real

    def inc(self, sem, k=1):
        self.p.cnt[sem] += k
        if self.real is not None:
            self.real.then_inc(self.p.semh[sem], k)
        return self


class _Eng:
    def __init__(self, p, name):
        self.p = p
        self.name = name

    def _real(self):
        return self.p.real_eng if self.p.emit_name == self.name else None

    def wait(self, sem, val):
        assert 0 <= val <= self.p.cnt[sem], (self.name, sem, val, self.p.cnt[sem])
        r = self._real()
        if r is not None and val > 0:
            r.wait_ge(self.p.semh[sem], val)

    def __getattr__(self, attr):
        r = self._real()
        if r is None:
            return lambda *a, **kw: _Instr(self.p, None)
        fn = getattr(r, attr)

        def wrap(*a, **kw):
            return _Instr(self.p, fn(*a, **kw))

        return wrap


class _Walk:
    def __init__(self, semh, emit_name, real_eng):
        self.semh = semh
        self.emit_name = emit_name
        self.real_eng = real_eng
        self.cnt = {k: 0 for k in semh}
        self.t = _Eng(self, "t")
        self.v = _Eng(self, "v")
        self.s = _Eng(self, "s")
        self.g = _Eng(self, "g")
        self.y = _Eng(self, "y")


def _build_nc():
    nc = bass.Bass("TRN2", target_bir_lowering=False, detect_race_conditions=False)

    x_h = nc.dram_tensor("x", [NB * T, D], F32, kind="ExternalInput")
    wqkv_h = nc.dram_tensor("w_qkv", [D, 3 * D], F32, kind="ExternalInput")
    bqkv_h = nc.dram_tensor("b_qkv", [1, 3 * D], F32, kind="ExternalInput")
    wproj_h = nc.dram_tensor("w_proj", [D, D], F32, kind="ExternalInput")
    bproj_h = nc.dram_tensor("b_proj", [1, D], F32, kind="ExternalInput")
    lut_h = nc.dram_tensor("lut", [1, 16], F32, kind="ExternalInput")
    gidxT_h = nc.dram_tensor("gidxT", [D, ITER], I32, kind="ExternalInput")
    uidxT_h = nc.dram_tensor("uidxT", [D, ITER], I32, kind="ExternalInput")
    didxT_h = nc.dram_tensor("didxT", [ITER, D], I32, kind="ExternalInput")
    ident_h = nc.dram_tensor("ident", [128, 128], F32, kind="ExternalInput")
    out_h = nc.dram_tensor("out", [NB * T, D], F32, kind="ExternalOutput")

    q_scr = nc.dram_tensor("q_scr", [NB * H * HD, T], BF16, kind="Internal")
    k_scr = nc.dram_tensor("k_scr", [NB * H * HD, T], BF16, kind="Internal")
    v_scr = nc.dram_tensor("v_scr", [NB * T, D], BF16, kind="Internal")
    ao_scr = nc.dram_tensor("ao_scr", [NB * DC * 128, T], BF16, kind="Internal")
    x1tm_scr = nc.dram_tensor("x1tm_scr", [NB * T, D], F32, kind="Internal")
    x1fm_scr = nc.dram_tensor("x1fm_scr", [NB * DC * 128, T], BF16, kind="Internal")
    x2_scr = nc.dram_tensor("x2_scr", [NB * T, D], F32, kind="Internal")
    wg_scr = nc.dram_tensor("wg_scr", [DC * 128, ITER], BF16, kind="Internal")
    wu_scr = nc.dram_tensor("wu_scr", [DC * 128, ITER], BF16, kind="Internal")
    wd_scr = nc.dram_tensor("wd_scr", [NIT * 128, D], BF16, kind="Internal")

    with ExitStack() as ctx:
        e = ctx.enter_context
        arena = e(nc.sbuf_tensor("arena", [128, AR_COLS], BF16))
        xst = [e(nc.sbuf_tensor(f"xst{i}", [128, D], F32)) for i in range(2)]
        pst = [e(nc.sbuf_tensor(f"pst{i}", [128, D], F32)) for i in range(2)]
        cst = [e(nc.sbuf_tensor(f"cst{i}", [128, D], BF16)) for i in range(2)]
        wst = e(nc.sbuf_tensor("wst", [128, 8192], BF16))
        wdst = [e(nc.sbuf_tensor(f"wdst{i}", [128, 512], BF16)) for i in range(2)]
        pidx = [e(nc.sbuf_tensor(f"pidx{i}", [128, D], BF16)) for i in range(PF)]
        pacc = [e(nc.sbuf_tensor(f"pacc{i}", [128, D], BF16)) for i in range(2)]
        ptmp = e(nc.sbuf_tensor("ptmp", [128, D], BF16))
        identb = e(nc.sbuf_tensor("identb", [128, 128], BF16))
        ones_r = e(nc.sbuf_tensor("ones_r", [1, 512], BF16))
        ones_c = e(nc.sbuf_tensor("ones_c", [128, 1], BF16))
        luts = e(nc.sbuf_tensor("luts", [1, 16], BF16))
        lutb = e(nc.sbuf_tensor("lutb", [128, 16], F32))
        stat = [e(nc.sbuf_tensor(f"stat{i}", [128, 1], F32)) for i in range(6)]
        c0col = e(nc.sbuf_tensor("c0col", [128, 1], F32))
        epscol = e(nc.sbuf_tensor("epscol", [128, 1], F32))
        pmean = e(nc.sbuf_tensor("pmean", [128, 32], F32))
        pssq = e(nc.sbuf_tensor("pssq", [128, 32], F32))
        PA = e(nc.psum_tensor("PA", [128, 1024], F32))
        PB = e(nc.psum_tensor("PB", [128, 1024], F32))
        PC = e(nc.psum_tensor("PC", [128, 1024], F32))
        PD = e(nc.psum_tensor("PD", [128, 1024], F32))
        semh = {}
        for sname in ALL_SEMS:
            semh[sname] = e(nc.semaphore(sname))
        block = e(nc.Block())

        bqr = arena[0:1, AR_QH:AR_QH + 3 * D]
        bpr = arena[0:1, AR_BPR:AR_BPR + D]
        recip = arena[0:1, AR_QKST:AR_QKST + 2048].bitcast(F32)
        recipb = arena[0:1, AR_RB:AR_RB + 1024]
        rcb = arena[:, AR_RC:AR_RC + 1024]
        PS4 = [PA[:, 0:512], PA[:, 512:1024], PB[:, 0:512], PB[:, 512:1024]]
        PS8 = PS4 + [PC[:, 0:512], PC[:, 512:1024], PD[:, 0:512], PD[:, 512:1024]]
        mean, ssq, var, std, rstd, mr = stat

        def walk(P: _Walk):
            t, v, s, g, y = P.t, P.v, P.s, P.g, P.y
            cnt = P.cnt
            mk = {}
            guards = {}

            def markv(semn):
                return (semn, cnt[semn])

            def wmark(eng, key):
                mv = mk[key]
                if isinstance(mv, list):
                    for semn, vv in mv:
                        eng.wait(semn, vv)
                else:
                    eng.wait(mv[0], mv[1])

            def add_guard(key, semn):
                guards.setdefault(key, []).append((semn, cnt[semn]))

            def guard(eng, key):
                for semn, valv in guards.pop(key, ()):
                    eng.wait(semn, valv)

            def pguard(eng, banks):
                for bk in banks:
                    guard(eng, ("pb", bk))

            def padd(banks, semn):
                for bk in banks:
                    add_guard(("pb", bk), semn)

            g.memset(ones_r[:, :], 1.0)
            g.memset(c0col[:, :], -C0)
            g.memset(epscol[:, :], EPS)
            g.memset(ones_c[:, :], 1.0).inc("gP", 1)
            g.dma_start(out=identb[:, :], in_=ident_h[:, :]).inc("lcs", 16)
            g.dma_start(out=luts[:, :], in_=lut_h[:, :]).inc("lcs", 16)
            t.wait("lcs", 32)
            t.wait("gP", 1)
            s.wait("gP", 1)
            t.matmul(PA[:, 0:16], ones_r[0:1, 0:128], luts[0:1, :],
                     start=True, stop=True).inc("tP", 1)
            mk["lutbc"] = markv("tP")
            wmark(v, "lutbc")
            v.tensor_copy(lutb[:, :], PA[:, 0:16]).inc("vD", 1)
            mk["lutb"] = markv("vD")
            v.wait("vD", cnt["vD"])
            padd([0], "vD")

            plan = []
            wchunks = [(0, 2048), (2048, 2048), (4096, 1408)]
            for scr, src in ((wg_scr, gidxT_h), (wu_scr, uidxT_h)):
                for (c0, w) in wchunks:
                    for hc in range(DC):
                        plan.append((scr, hc * 128, src, hc * 128, c0, w))
            for ic in range(NIT):
                plan.append((wd_scr, ic * 128, didxT_h, ic * 128, 0, 2048))
            n_prep = len(plan)
            pstate = {"comp": 0}

            def prep_load(k):
                scr, r0, src, sr0, c0, w = plan[k]
                guard(g, ("pidx", k % PF))
                g.dma_start(out=pidx[k % PF][:, 0:w],
                            in_=src[sr0:sr0 + 128, c0:c0 + w]).inc(LP[k % PF], 16)
                mk[("pload", k)] = markv(LP[k % PF])

            def prep_pump(n):
                for _ in range(n):
                    k = pstate["comp"]
                    if k >= n_prep:
                        return
                    scr, r0, src, sr0, c0, w = plan[k]
                    acc = pacc[k % 2]
                    wmark(v, ("pload", k))
                    guard(v, ("pacc", k % 2))
                    src_t = pidx[k % PF]
                    last = None
                    for j in range(16):
                        if j == 0:
                            last = v.tensor_scalar(
                                out=acc[:, 0:w], in0=src_t[:, 0:w],
                                scalar1=0.0, scalar2=lutb[:, 0:1],
                                op0=ALU.is_equal, op1=ALU.mult)
                        else:
                            v.tensor_scalar(
                                out=ptmp[:, 0:w], in0=src_t[:, 0:w],
                                scalar1=float(j), scalar2=lutb[:, j:j + 1],
                                op0=ALU.is_equal, op1=ALU.mult)
                            last = v.tensor_tensor(
                                out=acc[:, 0:w], in0=acc[:, 0:w],
                                in1=ptmp[:, 0:w], op=ALU.add)
                    last.inc("vD", 1)
                    mk[("ptile", k)] = markv("vD")
                    add_guard(("pidx", k % PF), "vD")
                    if k + PF < n_prep:
                        prep_load(k + PF)
                    wmark(s, ("ptile", k))
                    s.dma_start(out=scr[r0:r0 + 128, c0:c0 + w],
                                in_=acc[:, 0:w]).inc(SP_[k % 2], 16)
                    add_guard(("pacc", k % 2), SP_[k % 2])
                    mk[("pdone", id(scr), c0, r0)] = markv(SP_[k % 2])
                    if r0 == (DC - 1) * 128 and scr is not wd_scr:
                        mk[("chunk", id(scr), c0)] = [markv("sp0"), markv("sp1")]
                    pstate["comp"] = k + 1

            for k in range(min(PF, n_prep)):
                prep_load(k)

            def chunk_key(scr, col):
                key_c0 = 0
                for (c0, w) in wchunks:
                    if c0 <= col < c0 + w:
                        key_c0 = c0
                return ("chunk", id(scr), key_c0)

            for b in range(NB):
                for tt in range(DC):
                    if (b, tt) != (0, 0):
                        prep_pump(1)
                    par = tt % 2
                    guard(y, ("xst", par))
                    y.dma_start(out=xst[par][:, :],
                                in_=x_h[b * T + tt * 128: b * T + (tt + 1) * 128, :]
                                ).inc(LX[par], 16)
                    mk[("xld", b, tt)] = markv(LX[par])
                    wmark(s, ("xld", b, tt))
                    guard(s, ("cst", par))
                    s.copy(cst[par][:, :], xst[par][:, :]).inc("aA", 1)
                    add_guard(("xst", par), "aA")
                    mk[("xcast", b, tt)] = markv("aA")
                    ptr = (PA if tt % 2 == 0 else PB)[:, :].bitcast(BF16)
                    pbanks = [0, 1] if tt % 2 == 0 else [2, 3]
                    wmark(t, ("xcast", b, tt))
                    pguard(t, pbanks)
                    for dc in range(DC):
                        t.transpose(ptr[:, dc * 128:(dc + 1) * 128],
                                    cst[par][:, dc * 128:(dc + 1) * 128],
                                    identb[:, :])
                    t.matmul(PD[0:1, 1008:1024], ones_r[0:1, 0:1],
                             ones_r[0:1, 0:16], start=True, stop=True).inc("tP", 1)
                    add_guard(("cst", par), "tP")
                    mk[("xtr", b, tt)] = markv("tP")
                    wmark(v, ("xtr", b, tt))
                    if b > 0 and tt == 0:
                        guard(v, ("arena_xfm",))
                    dst = arena[:, AR_XFM:AR_XFM + 32768].rearrange(
                        "p (dc t) -> p dc t", dc=DC)[:, :, tt * 128:(tt + 1) * 128]
                    v.tensor_copy(dst, ptr.rearrange("p (dc t) -> p dc t", dc=DC)).inc("vD", 1)
                    padd(pbanks, "vD")
                mk[("xfm", b)] = markv("vD")

                if b > 0:
                    wmark(g, ("attn_done", b - 1))
                g.dma_start(out=bqr[0:1, :], in_=bqkv_h[:, :]).inc("lbq", 16)
                mk[("bqr", b)] = markv("lbq")
                for ot in range(2 * H):
                    prep_pump(1)
                    slot = ot % 4
                    guard(g, ("wst", slot))
                    g.dma_start(
                        out=wst[:, slot * 2048:(slot + 1) * 2048].rearrange(
                            "p (dc o) -> p dc o", dc=DC),
                        in_=wqkv_h[:, ot * 128:(ot + 1) * 128].rearrange(
                            "(dc p) o -> p dc o", p=128)).inc(LW[slot], 16)
                    mk[("wqk", b, ot)] = markv(LW[slot])
                    wmark(t, ("wqk", b, ot))
                    wmark(t, ("bqr", b))
                    wmark(t, ("xfm", b))
                    for tc in range(4):
                        pi = (ot * 4 + tc) % 4
                        iq = ot * 4 + tc
                        pguard(t, [pi])
                        for dc in range(DC):
                            t.matmul(PS4[pi],
                                     wst[:, slot * 2048 + dc * 128: slot * 2048 + (dc + 1) * 128],
                                     arena[:, AR_XFM + dc * 2048 + tc * 512:
                                           AR_XFM + dc * 2048 + (tc + 1) * 512],
                                     start=(dc == 0), stop=False)
                        t.matmul(PS4[pi], bqr[0:1, ot * 128:(ot + 1) * 128],
                                 ones_r[0:1, 0:512], start=False, stop=True).inc("tP", 1)
                        mk[("qkmmc", b, iq)] = markv("tP")
                        if tc == 3:
                            add_guard(("wst", slot), "tP")
                        stg = arena[:, AR_QKST + pi * 512: AR_QKST + (pi + 1) * 512]
                        wmark(s, ("qkmmc", b, iq))
                        guard(s, ("qkst", pi))
                        s.copy(stg, PS4[pi]).inc("aA", 1)
                        padd([pi], "aA")
                        mk[("qkev", b, iq)] = markv("aA")
                        scr = q_scr if ot < H else k_scr
                        hh = ot % H
                        wmark(y, ("qkev", b, iq))
                        y.dma_start(out=scr[(b * H + hh) * 128:(b * H + hh + 1) * 128,
                                            tc * 512:(tc + 1) * 512],
                                    in_=stg).inc(SQ[pi], 16)
                        add_guard(("qkst", pi), SQ[pi])

                for oc in range(4):
                    prep_pump(2)
                    for sl in range(4):
                        guard(g, ("wst", sl))
                    g.dma_start(
                        out=wst[:, 0:8192].rearrange("p (dc o) -> p dc o", dc=DC),
                        in_=wqkv_h[:, 2 * D + oc * 512: 2 * D + (oc + 1) * 512].rearrange(
                            "(dc p) o -> p dc o", p=128)).inc("lw0", 16)
                    mk[("wv", b, oc)] = markv("lw0")
                    wmark(t, ("wv", b, oc))
                    for tt in range(DC):
                        pi = (oc * DC + tt) % 4
                        iq = oc * DC + tt
                        pguard(t, [pi])
                        for dc in range(DC):
                            t.matmul(PS4[pi],
                                     arena[:, AR_XFM + dc * 2048 + tt * 128:
                                           AR_XFM + dc * 2048 + (tt + 1) * 128],
                                     wst[:, dc * 512:(dc + 1) * 512],
                                     start=(dc == 0), stop=False)
                        t.matmul(PS4[pi], ones_r[0:1, 0:128],
                                 bqr[0:1, 2 * D + oc * 512: 2 * D + (oc + 1) * 512],
                                 start=False, stop=True).inc("tP", 1)
                        mk[("vmmc", b, iq)] = markv("tP")
                        if tt == DC - 1:
                            for sl in range(4):
                                add_guard(("wst", sl), "tP")
                        stg = arena[:, AR_QKST + pi * 512: AR_QKST + (pi + 1) * 512]
                        wmark(s, ("vmmc", b, iq))
                        guard(s, ("qkst", pi))
                        s.copy(stg, PS4[pi]).inc("aA", 1)
                        padd([pi], "aA")
                        mk[("vev", b, iq)] = markv("aA")
                        wmark(y, ("vev", b, iq))
                        y.dma_start(out=v_scr[b * T + tt * 128: b * T + (tt + 1) * 128,
                                              oc * 512:(oc + 1) * 512],
                                    in_=stg).inc(SQ[pi], 16)
                        add_guard(("qkst", pi), SQ[pi])
                add_guard(("arena_xfm",), "tP")
                mk[("qkvst", b)] = [markv(sq) for sq in SQ]

                guard(g, ("arena_xfm",))
                add_guard(("arena_xfm",), "tP")
                for dc in range(DC):
                    g.dma_start(out=arena[:, AR_XFM + dc * 2048: AR_XFM + (dc + 1) * 2048],
                                in_=wproj_h[dc * 128:(dc + 1) * 128, :]).inc("lwp", 16)
                mk[("wproj", b)] = markv("lwp")

                for h in range(H):
                    prep_pump(2)
                    hp = h % 2
                    guard(y, ("attld", hp))
                    wmark(y, ("qkvst", b))
                    y.dma_start(out=arena[:, AR_QH + hp * 2048: AR_QH + (hp + 1) * 2048],
                                in_=q_scr[(b * H + h) * 128:(b * H + h + 1) * 128, :]
                                ).inc(LT[hp], 16)
                    y.dma_start(out=arena[:, AR_KH + hp * 2048: AR_KH + (hp + 1) * 2048],
                                in_=k_scr[(b * H + h) * 128:(b * H + h + 1) * 128, :]
                                ).inc(LT[hp], 16)
                    y.dma_start(out=arena[:, AR_VH + hp * 2048: AR_VH + (hp + 1) * 2048
                                          ].rearrange("p (kt o) -> p kt o", kt=DC),
                                in_=v_scr[b * T:(b + 1) * T, h * 128:(h + 1) * 128].rearrange(
                                    "(kt p) o -> p kt o", p=128)).inc(LT[hp], 16)
                    mk[("attld", b, h)] = markv(LT[hp])
                    qh = arena[:, AR_QH + hp * 2048: AR_QH + (hp + 1) * 2048]
                    kh = arena[:, AR_KH + hp * 2048: AR_KH + (hp + 1) * 2048]
                    vh = arena[:, AR_VH + hp * 2048: AR_VH + (hp + 1) * 2048]
                    for half in range(2):
                        wmark(t, ("attld", b, h))
                        pguard(t, [4, 5])
                        pguard(t, [6, 7])
                        last = None
                        for kt in range(DC):
                            Sp = PA if kt % 2 == 0 else PB
                            sbanks = [0, 1] if kt % 2 == 0 else [2, 3]
                            pguard(t, sbanks)
                            for qc in range(2):
                                t.matmul(Sp[:, qc * 512:(qc + 1) * 512],
                                         kh[:, kt * 128:(kt + 1) * 128],
                                         qh[:, half * 1024 + qc * 512:
                                            half * 1024 + (qc + 1) * 512],
                                         start=True, stop=True).inc("tP", 1)
                            mk[("score", b, h, half, kt)] = markv("tP")
                            pbt = arena[:, AR_PBT + (kt % 2) * 1024:
                                        AR_PBT + (kt % 2 + 1) * 1024]
                            wmark(s, ("score", b, h, half, kt))
                            guard(s, ("pbt", kt % 2))
                            s.activation(out=pbt, in_=Sp[:, 0:1024], func=AF.Exp,
                                         bias=c0col[:, 0:1], scale=SCALE).inc("aA", 1)
                            padd(sbanks, "aA")
                            mk[("exp", b, h, half, kt)] = markv("aA")
                            wmark(t, ("exp", b, h, half, kt))
                            for qc in range(2):
                                t.matmul(PD[0:1, qc * 512:(qc + 1) * 512], ones_c[:, 0:1],
                                         pbt[:, qc * 512:(qc + 1) * 512],
                                         start=(kt == 0), stop=(kt == DC - 1))
                            for qc in range(2):
                                last = t.matmul(PC[:, qc * 512:(qc + 1) * 512],
                                                vh[:, kt * 128:(kt + 1) * 128],
                                                pbt[:, qc * 512:(qc + 1) * 512],
                                                start=(kt == 0), stop=(kt == DC - 1))
                            last.inc("tP", 1)
                            add_guard(("pbt", kt % 2), "tP")
                            mk[("av", b, h, half, kt)] = markv("tP")
                        if half == 1:
                            add_guard(("attld", hp), "tP")
                        mk[("avdone2", b, h, half)] = markv("tP")
                        wmark(v, ("avdone2", b, h, half))
                        guard(v, ("recipb",))
                        v.reciprocal(recip[0:1, :], PD[0:1, 0:1024]).inc("vD", 1)
                        v.wait("vD", cnt["vD"])
                        v.tensor_copy(recipb[0:1, :], recip[0:1, :]).inc("vD", 1)
                        padd([6, 7], "vD")
                        mk[("recipb", b, h, half)] = markv("vD")
                        wmark(t, ("recipb", b, h, half))
                        pguard(t, [0, 1])
                        last = None
                        for qc in range(2):
                            last = t.matmul(PA[:, qc * 512:(qc + 1) * 512],
                                            ones_r[0:1, 0:128],
                                            recipb[0:1, qc * 512:(qc + 1) * 512],
                                            start=True, stop=True)
                        last.inc("tP", 1)
                        add_guard(("recipb",), "tP")
                        mk[("rbc", b, h, half)] = markv("tP")
                        wmark(v, ("rbc", b, h, half))
                        v.tensor_copy(rcb[:, 0:1024], PA[:, 0:1024]).inc("vD", 1)
                        padd([0, 1], "vD")
                        v.wait("vD", cnt["vD"])
                        for qc in range(2):
                            stg = arena[:, AR_AVST + qc * 512: AR_AVST + (qc + 1) * 512]
                            guard(v, ("avst", qc))
                            v.tensor_tensor(out=stg, in0=PC[:, qc * 512:(qc + 1) * 512],
                                            in1=rcb[:, qc * 512:(qc + 1) * 512],
                                            op=ALU.mult).inc("vD", 1)
                            mk[("avev_c", b, h, half, qc)] = markv("vD")
                            if qc == 1:
                                padd([4, 5], "vD")
                            wmark(y, ("avev_c", b, h, half, qc))
                            y.dma_start(
                                out=ao_scr[(b * H + h) * 128:(b * H + h + 1) * 128,
                                           half * 1024 + qc * 512:
                                           half * 1024 + (qc + 1) * 512],
                                in_=stg).inc(SV[qc], 16)
                            add_guard(("avst", qc), SV[qc])
                mk[("ao_all", b)] = [markv(svn) for svn in SV]
                mk[("attn_done", b)] = markv("tP")

                wmark(g, ("attn_done", b))
                g.dma_start(out=bpr[0:1, :], in_=bproj_h[:, :]).inc("lbp", 16)
                mk[("bpr", b)] = markv("lbp")
                for tt in range(DC):
                    prep_pump(2)
                    par = tt % 2
                    guard(y, ("xst", par))
                    y.dma_start(out=xst[par][:, :],
                                in_=x_h[b * T + tt * 128: b * T + (tt + 1) * 128, :]
                                ).inc(LX[par], 16)
                    mk[("res", b, tt)] = markv(LX[par])
                    slot = tt % 4
                    guard(g, ("wst", slot))
                    wmark(g, ("ao_all", b))
                    g.dma_start(out=wst[:, slot * 2048:(slot + 1) * 2048].rearrange(
                                    "p (dc t) -> p dc t", dc=DC),
                                in_=ao_scr[b * DC * 128:(b + 1) * DC * 128,
                                           tt * 128:(tt + 1) * 128].rearrange(
                                    "(dc p) t -> p dc t", p=128)).inc(LW[slot], 16)
                    mk[("aold", b, tt)] = markv(LW[slot])
                    wmark(t, ("aold", b, tt))
                    wmark(t, ("wproj", b))
                    wmark(t, ("bpr", b))
                    for oc in range(4):
                        pi = (tt * 4 + oc) % 4
                        pguard(t, [pi])
                        for dc in range(DC):
                            t.matmul(PS4[pi],
                                     wst[:, slot * 2048 + dc * 128:
                                         slot * 2048 + (dc + 1) * 128],
                                     arena[:, AR_XFM + dc * 2048 + oc * 512:
                                           AR_XFM + dc * 2048 + (oc + 1) * 512],
                                     start=(dc == 0), stop=False)
                        t.matmul(PS4[pi], ones_r[0:1, 0:128],
                                 bpr[0:1, oc * 512:(oc + 1) * 512],
                                 start=False, stop=True).inc("tP", 1)
                        mk[("projc", b, tt, oc)] = markv("tP")
                        if oc == 3:
                            add_guard(("wst", slot), "tP")
                            add_guard(("arena_xfm",), "tP")
                        wmark(v, ("projc", b, tt, oc))
                        wmark(v, ("res", b, tt))
                        if oc == 0:
                            guard(v, ("pst", par))
                        v.tensor_tensor(out=pst[par][:, oc * 512:(oc + 1) * 512],
                                        in0=PS4[pi],
                                        in1=xst[par][:, oc * 512:(oc + 1) * 512],
                                        op=ALU.add).inc("vD", 1)
                        if oc == 3:
                            add_guard(("xst", par), "vD")
                        padd([pi], "vD")
                        mk[("projadd", b, tt, oc)] = markv("vD")
                    guard(v, ("cst", par))
                    v.wait("vD", cnt["vD"])
                    v.tensor_scalar(out=cst[par][:, :], in0=pst[par][:, :],
                                    scalar1=RD, scalar2=None, op0=ALU.mult,
                                    op1=ALU.add, accum_out=mean[:, 0:1]).inc("vD", 1)
                    mk[("lnmean", b, tt)] = markv("vD")
                    v.wait("vD", cnt["vD"])
                    wmark(s, ("lnmean", b, tt))
                    guard(s, ("trst",))
                    s.activation(out=arena[:, AR_TRST:AR_TRST + 2048],
                                 in_=pst[par][:, :], func=AF.Square, scale=RSD,
                                 accum_out=ssq[:, 0:1]).inc("aA", 1)
                    mk[("lnssq", b, tt)] = markv("aA")
                    wmark(v, ("lnssq", b, tt))
                    v.scalar_tensor_tensor(out=var[:, 0:1], in0=mean[:, 0:1],
                                           scalar=mean[:, 0:1], in1=ssq[:, 0:1],
                                           op0=ALU.mult, op1=ALU.subtract).inc("vD", 1)
                    mk[("lnvar", b, tt)] = markv("vD")
                    wmark(s, ("lnvar", b, tt))
                    s.activation(out=std[:, 0:1], in_=var[:, 0:1], func=AF.Sqrt,
                                 bias=epscol[:, 0:1], scale=-1.0).inc("aA", 1)
                    mk[("lnstd", b, tt)] = markv("aA")
                    wmark(v, ("lnstd", b, tt))
                    v.reciprocal(rstd[:, 0:1], std[:, 0:1]).inc("vD", 1)
                    v.wait("vD", cnt["vD"])
                    v.tensor_scalar(out=pst[par][:, :], in0=pst[par][:, :],
                                    scalar1=mean[:, 0:1], scalar2=rstd[:, 0:1],
                                    op0=ALU.subtract, op1=ALU.mult).inc("vD", 1)
                    mk[("x1", b, tt)] = markv("vD")
                    wmark(y, ("x1", b, tt))
                    y.dma_start(out=x1tm_scr[b * T + tt * 128: b * T + (tt + 1) * 128, :],
                                in_=pst[par][:, :]).inc(SM[par], 16)
                    add_guard(("pst", par), SM[par])
                    wmark(s, ("x1", b, tt))
                    guard(s, ("cst", par))
                    s.copy(cst[par][:, :], pst[par][:, :]).inc("aA", 1)
                    mk[("x1bf", b, tt)] = markv("aA")
                    guard(s, ("trst",))
                    for dc in range(DC):
                        s.dma_start_transpose(
                            out=arena[:, AR_TRST + dc * 128: AR_TRST + (dc + 1) * 128],
                            in_=cst[par][:, dc * 128:(dc + 1) * 128]).inc("sxt", 16)
                    add_guard(("cst", par), "sxt")
                    mk[("x1trd", b, tt)] = markv("sxt")
                    wmark(s, ("x1trd", b, tt))
                    s.dma_start(out=x1fm_scr[b * DC * 128:(b + 1) * DC * 128,
                                             tt * 128:(tt + 1) * 128].rearrange(
                                    "(dc p) t -> p dc t", p=128),
                                in_=arena[:, AR_TRST:AR_TRST + 2048].rearrange(
                                    "p (dc t) -> p dc t", dc=DC)).inc("sxf", 16)
                    add_guard(("trst",), "sxf")
                mk[("x1_all", b)] = markv("sxf")
                mk[("x1tm_all", b)] = [markv(smn) for smn in SM]
                mk[("projmm_done", b)] = markv("tP")

            for b in range(NB):
                for ch in range(2):
                    cidx = b * 2 + ch
                    guard(y, ("ar_x1fm",))
                    wmark(y, ("x1_all", NB - 1))
                    y.dma_start(out=arena[:, AR_X1FM:AR_X1FM + 16384].rearrange(
                                    "p (dc t) -> p dc t", dc=DC),
                                in_=x1fm_scr[b * DC * 128:(b + 1) * DC * 128,
                                             ch * 1024:(ch + 1) * 1024].rearrange(
                                    "(dc p) t -> p dc t", p=128)).inc("lxf", 16)
                    mk[("x1fmld", cidx)] = markv("lxf")

                    for it in range(NIT):
                        if pstate["comp"] < n_prep:
                            prep_pump(1)
                        wgslot = it % 2
                        wuslot = 2 + it % 2
                        guard(g, ("wst", wgslot))
                        wmark(g, chunk_key(wg_scr, it * 128))
                        g.dma_start(out=wst[:, wgslot * 2048:(wgslot + 1) * 2048].rearrange(
                                        "p (hc i) -> p hc i", hc=DC),
                                    in_=wg_scr[:, it * 128:(it + 1) * 128].rearrange(
                                        "(hc p) i -> p hc i", p=128)).inc(LW[wgslot], 16)
                        mk[("wgld", cidx, it)] = markv(LW[wgslot])
                        guard(g, ("wst", wuslot))
                        wmark(g, chunk_key(wu_scr, it * 128))
                        g.dma_start(out=wst[:, wuslot * 2048:(wuslot + 1) * 2048].rearrange(
                                        "p (hc i) -> p hc i", hc=DC),
                                    in_=wu_scr[:, it * 128:(it + 1) * 128].rearrange(
                                        "(hc p) i -> p hc i", p=128)).inc(LW[wuslot], 16)
                        mk[("wuld", cidx, it)] = markv(LW[wuslot])
                        Gp, gbanks = (PA, [0, 1]) if it % 2 == 0 else (PC, [4, 5])
                        Up, ubanks = (PB, [2, 3]) if it % 2 == 0 else (PD, [6, 7])
                        wmark(t, ("wgld", cidx, it))
                        wmark(t, ("x1fmld", cidx))
                        pguard(t, gbanks)
                        last = None
                        for tc2 in range(2):
                            for hc in range(DC):
                                last = t.matmul(
                                    Gp[:, tc2 * 512:(tc2 + 1) * 512],
                                    wst[:, wgslot * 2048 + hc * 128:
                                        wgslot * 2048 + (hc + 1) * 128],
                                    arena[:, AR_X1FM + hc * 1024 + tc2 * 512:
                                          AR_X1FM + hc * 1024 + (tc2 + 1) * 512],
                                    start=(hc == 0), stop=(hc == DC - 1))
                        last.inc("tP", 1)
                        add_guard(("wst", wgslot), "tP")
                        mk[("gmm", cidx, it)] = markv("tP")
                        wmark(t, ("wuld", cidx, it))
                        pguard(t, ubanks)
                        for tc2 in range(2):
                            for hc in range(DC):
                                last = t.matmul(
                                    Up[:, tc2 * 512:(tc2 + 1) * 512],
                                    wst[:, wuslot * 2048 + hc * 128:
                                        wuslot * 2048 + (hc + 1) * 128],
                                    arena[:, AR_X1FM + hc * 1024 + tc2 * 512:
                                          AR_X1FM + hc * 1024 + (tc2 + 1) * 512],
                                    start=(hc == 0), stop=(hc == DC - 1))
                        last.inc("tP", 1)
                        add_guard(("wst", wuslot), "tP")
                        if it == NIT - 1:
                            add_guard(("ar_x1fm",), "tP")
                        mk[("umm", cidx, it)] = markv("tP")
                        gbs = arena[:, AR_GB + (it % 2) * 1024: AR_GB + (it % 2 + 1) * 1024]
                        wmark(s, ("gmm", cidx, it))
                        guard(s, ("gb", it % 2))
                        s.activation(out=gbs, in_=Gp[:, 0:1024], func=AF.Silu,
                                     bias=0.0, scale=1.0).inc("aA", 1)
                        padd(gbanks, "aA")
                        mk[("silu", cidx, it)] = markv("aA")
                        wmark(v, ("umm", cidx, it))
                        wmark(v, ("silu", cidx, it))
                        if it == 0:
                            guard(v, ("ar_h",))
                        v.tensor_tensor(out=arena[:, AR_H + it * 1024: AR_H + (it + 1) * 1024],
                                        in0=Up[:, 0:1024], in1=gbs, op=ALU.mult).inc("vD", 1)
                        padd(ubanks, "vD")
                        add_guard(("gb", it % 2), "vD")
                        mk[("hmul", cidx, it)] = markv("vD")

                    for oc in range(4):
                        for ic in range(NIT):
                            wp = ic % 2
                            guard(g, ("wdst", wp))
                            if cidx == 0 and oc == 0:
                                wmark(g, ("pdone", id(wd_scr), 0, ic * 128))
                            g.dma_start(out=wdst[wp][:, :],
                                        in_=wd_scr[ic * 128:(ic + 1) * 128,
                                                   oc * 512:(oc + 1) * 512]).inc(LD_[wp], 16)
                            mk[("wdld", cidx, oc, ic)] = markv(LD_[wp])
                            wmark(t, ("wdld", cidx, oc, ic))
                            if ic == 0:
                                wmark(t, ("hmul", cidx, NIT - 1))
                                for tt2 in range(8):
                                    pguard(t, [tt2])
                            last = None
                            for tt2 in range(8):
                                last = t.matmul(
                                    PS8[tt2],
                                    arena[:, AR_H + ic * 1024 + tt2 * 128:
                                          AR_H + ic * 1024 + (tt2 + 1) * 128],
                                    wdst[wp][:, :],
                                    start=(ic == 0), stop=(ic == NIT - 1))
                            last.inc("tP", 1)
                            add_guard(("wdst", wp), "tP")
                        mk[("dmmoc", cidx, oc)] = markv("tP")
                        if oc == 3:
                            add_guard(("ar_h",), "tP")
                        for tt2 in range(8):
                            gtile = b * T + ch * 1024 + tt2 * 128
                            par = tt2 % 2
                            guard(y, ("xst", par))
                            wmark(y, ("x1tm_all", b))
                            y.dma_start(out=xst[par][:, oc * 512:(oc + 1) * 512],
                                        in_=x1tm_scr[gtile:gtile + 128,
                                                     oc * 512:(oc + 1) * 512]
                                        ).inc(LX[par], 16)
                            mk[("x1res", cidx, oc, tt2)] = markv(LX[par])
                            si = par * 4 + ((tt2 // 2) % 2) * 2 + oc % 2
                            stg = pst[par][:, ((tt2 // 2) % 2) * 1024 + (oc % 2) * 512:
                                           ((tt2 // 2) % 2) * 1024 + (oc % 2 + 1) * 512]
                            wmark(v, ("dmmoc", cidx, oc))
                            wmark(v, ("x1res", cidx, oc, tt2))
                            if cidx == 0 and oc == 0:
                                wmark(v, ("x1tm_all", NB - 1))
                            guard(v, ("stg", si))
                            guard(v, ("cst", par))
                            v.tensor_tensor(out=stg, in0=PS8[tt2],
                                            in1=xst[par][:, oc * 512:(oc + 1) * 512],
                                            op=ALU.add).inc("vD", 1)
                            v.wait("vD", cnt["vD"])
                            v.tensor_scalar(out=cst[par][:, 0:512], in0=stg,
                                            scalar1=RD, scalar2=None, op0=ALU.mult,
                                            op1=ALU.add,
                                            accum_out=pmean[:, tt2 * 4 + oc:
                                                            tt2 * 4 + oc + 1]).inc("vD", 1)
                            padd([tt2], "vD")
                            add_guard(("xst", par), "vD")
                            mk[("dadd", cidx, oc, tt2)] = markv("vD")
                            wmark(s, ("dadd", cidx, oc, tt2))
                            s.activation(out=cst[par][:, 512:1024], in_=stg,
                                         func=AF.Square, scale=RSD,
                                         accum_out=pssq[:, tt2 * 4 + oc:
                                                        tt2 * 4 + oc + 1]).inc("aA", 1)
                            mk[("dsq", cidx, oc, tt2)] = markv("aA")
                            s.dma_start(out=x2_scr[gtile:gtile + 128,
                                                   oc * 512:(oc + 1) * 512],
                                        in_=stg).inc(SG[si], 16)
                            add_guard(("stg", si), SG[si])
                            mk[("x2st", cidx, oc, tt2)] = markv(SG[si])

                    for tt2 in range(8):
                        gtile = b * T + ch * 1024 + tt2 * 128
                        par = tt2 % 2
                        guard(y, ("xst", par))
                        for oc in range(4):
                            wmark(y, ("x2st", cidx, oc, tt2))
                        y.dma_start(out=xst[par][:, :],
                                    in_=x2_scr[gtile:gtile + 128, :]).inc(LX[par], 16)
                        mk[("x2ld", cidx, tt2)] = markv(LX[par])
                        wmark(v, ("x2ld", cidx, tt2))
                        wmark(v, ("dsq", cidx, 3, tt2))
                        v.wait("vD", cnt["vD"])
                        v.tensor_reduce(mean[:, 0:1], pmean[:, tt2 * 4:(tt2 + 1) * 4],
                                        axis=mybir.AxisListType.X, op=ALU.add)
                        v.tensor_reduce(ssq[:, 0:1], pssq[:, tt2 * 4:(tt2 + 1) * 4],
                                        axis=mybir.AxisListType.X, op=ALU.add).inc("vD", 1)
                        v.wait("vD", cnt["vD"])
                        v.scalar_tensor_tensor(out=var[:, 0:1], in0=mean[:, 0:1],
                                               scalar=mean[:, 0:1], in1=ssq[:, 0:1],
                                               op0=ALU.mult, op1=ALU.subtract).inc("vD", 1)
                        mk[("fvar", cidx, tt2)] = markv("vD")
                        wmark(s, ("fvar", cidx, tt2))
                        s.activation(out=std[:, 0:1], in_=var[:, 0:1], func=AF.Sqrt,
                                     bias=epscol[:, 0:1], scale=-1.0).inc("aA", 1)
                        mk[("fstd", cidx, tt2)] = markv("aA")
                        wmark(v, ("fstd", cidx, tt2))
                        v.reciprocal(rstd[:, 0:1], std[:, 0:1]).inc("vD", 1)
                        v.wait("vD", cnt["vD"])
                        v.tensor_scalar(out=xst[par][:, :], in0=xst[par][:, :],
                                        scalar1=mean[:, 0:1], scalar2=rstd[:, 0:1],
                                        op0=ALU.subtract, op1=ALU.mult).inc("vD", 1)
                        mk[("outn", cidx, tt2)] = markv("vD")
                        wmark(y, ("outn", cidx, tt2))
                        y.dma_start(out=out_h[gtile:gtile + 128, :],
                                    in_=xst[par][:, :]).inc(SO[par], 16)
                        add_guard(("xst", par), SO[par])

            assert pstate["comp"] == n_prep, (pstate["comp"], n_prep)
            for semn in DMA_SEMS:
                y.wait(semn, cnt[semn])

        @block.tensor
        def _(eng):
            walk(_Walk(semh, "t", eng))

        @block.vector
        def _(eng):
            walk(_Walk(semh, "v", eng))

        @block.scalar
        def _(eng):
            walk(_Walk(semh, "s", eng))

        @block.gpsimd
        def _(eng):
            walk(_Walk(semh, "g", eng))

        @block.sync
        def _(eng):
            walk(_Walk(semh, "y", eng))

    return nc


def kernel(L0_attn_x, L0_attn_w_qkv, L0_attn_b_qkv, L0_attn_w_proj, L0_attn_b_proj,
           L0_ffn_lut, L0_ffn_gate_idx, L0_ffn_up_idx, L0_ffn_down_idx):
    x = np.ascontiguousarray(np.asarray(L0_attn_x, dtype=np.float32))
    w_qkv = np.ascontiguousarray(np.asarray(L0_attn_w_qkv, dtype=np.float32))
    b_qkv = np.ascontiguousarray(np.asarray(L0_attn_b_qkv, dtype=np.float32).reshape(1, 3 * D))
    w_proj = np.ascontiguousarray(np.asarray(L0_attn_w_proj, dtype=np.float32))
    b_proj = np.ascontiguousarray(np.asarray(L0_attn_b_proj, dtype=np.float32).reshape(1, D))
    lut = np.ascontiguousarray(np.asarray(L0_ffn_lut, dtype=np.float32).reshape(1, 16))
    gidxT = np.ascontiguousarray(np.asarray(L0_ffn_gate_idx, dtype=np.int32).T)
    uidxT = np.ascontiguousarray(np.asarray(L0_ffn_up_idx, dtype=np.int32).T)
    didxT = np.ascontiguousarray(np.asarray(L0_ffn_down_idx, dtype=np.int32).T)
    ident = np.eye(128, dtype=np.float32)

    nc = _build_nc()
    in_maps = []
    for c in range(8):
        xs = np.ascontiguousarray(x[2 * c:2 * c + 2].reshape(NB * T, D))
        in_maps.append({
            "x": xs, "w_qkv": w_qkv, "b_qkv": b_qkv, "w_proj": w_proj,
            "b_proj": b_proj, "lut": lut, "gidxT": gidxT, "uidxT": uidxT,
            "didxT": didxT, "ident": ident,
        })
    res = run_bass_kernel_spmd(nc, in_maps, core_ids=list(range(8)))
    out = np.concatenate([r["out"].reshape(NB, T, D) for r in res.results], axis=0)
    return out


# version 02
# speedup vs baseline: 1.1900x; 1.1900x over previous
from contextlib import ExitStack

import numpy as np

import concourse.bass as bass
import concourse.mybir as mybir
from concourse.bass_utils import run_bass_kernel_spmd

F32 = mybir.dt.float32
BF16 = mybir.dt.bfloat16
I32 = mybir.dt.int32
AF = mybir.ActivationFunctionType
ALU = mybir.AluOpType

D = 2048
DC = 16
T = 2048
NB = 2
H = 16
HD = 128
ITER = 5504
NIT = 43
SCALE = 1.0 / float(np.sqrt(HD))
C0 = 12.0
EPS = 1e-5
RD = 1.0 / D
RSD = 1.0 / float(np.sqrt(D))

AR_XFM = 0
AR_QKST = 32768
AR_QH = 34816
AR_KH = 38912
AR_VH = 43008
AR_PBT = 47104
AR_AVST = 49152
AR_RC = 50176
AR_RB = 51200
AR_TRST = 52224
AR_BPR = 54272
AR_H = 0
AR_X1FM = 44032
AR_GB = 60416
AR_COLS = 62464

PF = 3

DMA_SEMS = (
    ["lxa", "lxb", "lw0", "lw1", "lw2", "lw3", "ld0", "ld1", "lp0", "lp1", "lp2",
     "lt0", "lt1", "lxf", "lwp", "lbq", "lbp", "lcs"]
    + ["sq0", "sq1", "sq2", "sq3", "sv0", "sv1", "sm0", "sm1"]
    + [f"sg{i}" for i in range(8)]
    + ["sxf", "sp0", "sp1", "so0", "so1", "sxa", "sxt"]
)
ALL_SEMS = DMA_SEMS + ["tP", "vD", "aA", "gP"]
LX = ["lxa", "lxb"]
LW = ["lw0", "lw1", "lw2", "lw3"]
LD_ = ["ld0", "ld1"]
LP = ["lp0", "lp1", "lp2"]
LT = ["lt0", "lt1"]
SQ = ["sq0", "sq1", "sq2", "sq3"]
SV = ["sv0", "sv1"]
SM = ["sm0", "sm1"]
SG = [f"sg{i}" for i in range(8)]
SP_ = ["sp0", "sp1"]
SO = ["so0", "so1"]


class _Instr:
    __slots__ = ("p", "real")

    def __init__(self, p, real):
        self.p = p
        self.real = real

    def inc(self, sem, k=1):
        self.p.cnt[sem] += k
        if self.real is not None:
            self.real.then_inc(self.p.semh[sem], k)
        return self


class _Eng:
    def __init__(self, p, name):
        self.p = p
        self.name = name

    def _real(self):
        return self.p.real_eng if self.p.emit_name == self.name else None

    def wait(self, sem, val):
        assert 0 <= val <= self.p.cnt[sem], (self.name, sem, val, self.p.cnt[sem])
        r = self._real()
        if r is not None and val > 0:
            r.wait_ge(self.p.semh[sem], val)

    def __getattr__(self, attr):
        r = self._real()
        if r is None:
            return lambda *a, **kw: _Instr(self.p, None)
        fn = getattr(r, attr)

        def wrap(*a, **kw):
            return _Instr(self.p, fn(*a, **kw))

        return wrap


class _Walk:
    def __init__(self, semh, emit_name, real_eng):
        self.semh = semh
        self.emit_name = emit_name
        self.real_eng = real_eng
        self.cnt = {k: 0 for k in semh}
        self.t = _Eng(self, "t")
        self.v = _Eng(self, "v")
        self.s = _Eng(self, "s")
        self.g = _Eng(self, "g")
        self.y = _Eng(self, "y")


def _build_nc():
    nc = bass.Bass("TRN2", target_bir_lowering=False, detect_race_conditions=False)

    x_h = nc.dram_tensor("x", [NB * T, D], F32, kind="ExternalInput")
    wqkv_h = nc.dram_tensor("w_qkv", [D, 3 * D], F32, kind="ExternalInput")
    bqkv_h = nc.dram_tensor("b_qkv", [1, 3 * D], F32, kind="ExternalInput")
    wproj_h = nc.dram_tensor("w_proj", [D, D], F32, kind="ExternalInput")
    bproj_h = nc.dram_tensor("b_proj", [1, D], F32, kind="ExternalInput")
    lut_h = nc.dram_tensor("lut", [1, 16], F32, kind="ExternalInput")
    gidxT_h = nc.dram_tensor("gidxT", [D, ITER], I32, kind="ExternalInput")
    uidxT_h = nc.dram_tensor("uidxT", [D, ITER], I32, kind="ExternalInput")
    didxT_h = nc.dram_tensor("didxT", [ITER, D], I32, kind="ExternalInput")
    ident_h = nc.dram_tensor("ident", [128, 128], F32, kind="ExternalInput")
    out_h = nc.dram_tensor("out", [NB * T, D], F32, kind="ExternalOutput")

    q_scr = nc.dram_tensor("q_scr", [NB * H * HD, T], BF16, kind="Internal")
    k_scr = nc.dram_tensor("k_scr", [NB * H * HD, T], BF16, kind="Internal")
    v_scr = nc.dram_tensor("v_scr", [NB * T, D], BF16, kind="Internal")
    ao_scr = nc.dram_tensor("ao_scr", [NB * DC * 128, T], BF16, kind="Internal")
    x1tm_scr = nc.dram_tensor("x1tm_scr", [NB * T, D], F32, kind="Internal")
    x1fm_scr = nc.dram_tensor("x1fm_scr", [NB * DC * 128, T], BF16, kind="Internal")
    x2_scr = nc.dram_tensor("x2_scr", [NB * T, D], F32, kind="Internal")
    wg_scr = nc.dram_tensor("wg_scr", [DC * 128, ITER], BF16, kind="Internal")
    wu_scr = nc.dram_tensor("wu_scr", [DC * 128, ITER], BF16, kind="Internal")
    wd_scr = nc.dram_tensor("wd_scr", [NIT * 128, D], BF16, kind="Internal")

    with ExitStack() as ctx:
        e = ctx.enter_context
        arena = e(nc.sbuf_tensor("arena", [128, AR_COLS], BF16))
        xst = [e(nc.sbuf_tensor(f"xst{i}", [128, D], F32)) for i in range(2)]
        pst = [e(nc.sbuf_tensor(f"pst{i}", [128, D], F32)) for i in range(2)]
        cst = [e(nc.sbuf_tensor(f"cst{i}", [128, D], BF16)) for i in range(2)]
        wst = e(nc.sbuf_tensor("wst", [128, 8192], BF16))
        wdst = [e(nc.sbuf_tensor(f"wdst{i}", [128, 512], BF16)) for i in range(2)]
        pidx = [e(nc.sbuf_tensor(f"pidx{i}", [128, D], BF16)) for i in range(PF)]
        pacc = [e(nc.sbuf_tensor(f"pacc{i}", [128, D], BF16)) for i in range(2)]
        ptmp = e(nc.sbuf_tensor("ptmp", [128, D], BF16))
        identb = e(nc.sbuf_tensor("identb", [128, 128], BF16))
        ones_r = e(nc.sbuf_tensor("ones_r", [1, 512], BF16))
        ones_c = e(nc.sbuf_tensor("ones_c", [128, 1], BF16))
        luts = e(nc.sbuf_tensor("luts", [1, 16], BF16))
        lutb = e(nc.sbuf_tensor("lutb", [128, 16], F32))
        stat = [e(nc.sbuf_tensor(f"stat{i}", [128, 1], F32)) for i in range(6)]
        c0col = e(nc.sbuf_tensor("c0col", [128, 1], F32))
        epscol = e(nc.sbuf_tensor("epscol", [128, 1], F32))
        pmean = e(nc.sbuf_tensor("pmean", [128, 32], F32))
        pssq = e(nc.sbuf_tensor("pssq", [128, 32], F32))
        PA = e(nc.psum_tensor("PA", [128, 1024], F32))
        PB = e(nc.psum_tensor("PB", [128, 1024], F32))
        PC = e(nc.psum_tensor("PC", [128, 1024], F32))
        PD = e(nc.psum_tensor("PD", [128, 1024], F32))
        semh = {}
        for sname in ALL_SEMS:
            semh[sname] = e(nc.semaphore(sname))
        block = e(nc.Block())

        bqr = arena[0:1, AR_QH:AR_QH + 3 * D]
        bpr = arena[0:1, AR_BPR:AR_BPR + D]
        recip = arena[0:1, AR_QKST:AR_QKST + 2048].bitcast(F32)
        recipb = arena[0:1, AR_RB:AR_RB + 1024]
        rcb = arena[:, AR_RC:AR_RC + 1024]
        PS4 = [PA[:, 0:512], PA[:, 512:1024], PB[:, 0:512], PB[:, 512:1024]]
        PS8 = PS4 + [PC[:, 0:512], PC[:, 512:1024], PD[:, 0:512], PD[:, 512:1024]]
        mean, ssq, var, std, rstd, mr = stat

        def walk(P: _Walk):
            t, v, s, g, y = P.t, P.v, P.s, P.g, P.y
            cnt = P.cnt
            mk = {}
            guards = {}

            def markv(semn):
                return (semn, cnt[semn])

            def wmark(eng, key):
                mv = mk[key]
                if isinstance(mv, list):
                    for semn, vv in mv:
                        eng.wait(semn, vv)
                else:
                    eng.wait(mv[0], mv[1])

            def add_guard(key, semn):
                guards.setdefault(key, []).append((semn, cnt[semn]))

            def guard(eng, key):
                for semn, valv in guards.pop(key, ()):
                    eng.wait(semn, valv)

            def pguard(eng, banks):
                for bk in banks:
                    guard(eng, ("pb", bk))

            def padd(banks, semn):
                for bk in banks:
                    add_guard(("pb", bk), semn)

            g.memset(ones_r[:, :], 1.0)
            g.memset(c0col[:, :], -C0)
            g.memset(epscol[:, :], EPS)
            g.memset(ones_c[:, :], 1.0).inc("gP", 1)
            g.dma_start(out=identb[:, :], in_=ident_h[:, :]).inc("lcs", 16)
            g.dma_start(out=luts[:, :], in_=lut_h[:, :]).inc("lcs", 16)
            t.wait("lcs", 32)
            t.wait("gP", 1)
            s.wait("gP", 1)
            t.matmul(PA[:, 0:16], ones_r[0:1, 0:128], luts[0:1, :],
                     start=True, stop=True).inc("tP", 1)
            mk["lutbc"] = markv("tP")
            wmark(v, "lutbc")
            v.tensor_copy(lutb[:, :], PA[:, 0:16]).inc("vD", 1)
            mk["lutb"] = markv("vD")
            v.wait("vD", cnt["vD"])
            padd([0], "vD")

            plan = []
            wchunks = [(0, 2048), (2048, 2048), (4096, 1408)]
            for scr, src in ((wg_scr, gidxT_h), (wu_scr, uidxT_h)):
                for (c0, w) in wchunks:
                    for hc in range(DC):
                        plan.append((scr, hc * 128, src, hc * 128, c0, w))
            for ic in range(NIT):
                plan.append((wd_scr, ic * 128, didxT_h, ic * 128, 0, 2048))
            n_prep = len(plan)
            pstate = {"comp": 0}

            def prep_load(k):
                scr, r0, src, sr0, c0, w = plan[k]
                if k >= PF:
                    wmark(g, ("ptile", k - PF))
                g.dma_start(out=pidx[k % PF][:, 0:w],
                            in_=src[sr0:sr0 + 128, c0:c0 + w]).inc(LP[k % PF], 16)
                mk[("pload", k)] = markv(LP[k % PF])

            def prep_pump(n):
                for _ in range(n):
                    k = pstate["comp"]
                    if k >= n_prep:
                        return
                    scr, r0, src, sr0, c0, w = plan[k]
                    acc = pacc[k % 2]
                    wmark(v, ("pload", k))
                    guard(v, ("pacc", k % 2))
                    src_t = pidx[k % PF]
                    last = None
                    for j in range(16):
                        if j == 0:
                            last = v.tensor_scalar(
                                out=acc[:, 0:w], in0=src_t[:, 0:w],
                                scalar1=0.0, scalar2=lutb[:, 0:1],
                                op0=ALU.is_equal, op1=ALU.mult)
                        else:
                            v.tensor_scalar(
                                out=ptmp[:, 0:w], in0=src_t[:, 0:w],
                                scalar1=float(j), scalar2=lutb[:, j:j + 1],
                                op0=ALU.is_equal, op1=ALU.mult)
                            last = v.tensor_tensor(
                                out=acc[:, 0:w], in0=acc[:, 0:w],
                                in1=ptmp[:, 0:w], op=ALU.add)
                    last.inc("vD", 1)
                    mk[("ptile", k)] = markv("vD")
                    if k + PF < n_prep:
                        prep_load(k + PF)
                    wmark(g, ("ptile", k))
                    g.dma_start(out=scr[r0:r0 + 128, c0:c0 + w],
                                in_=acc[:, 0:w]).inc(SP_[k % 2], 16)
                    add_guard(("pacc", k % 2), SP_[k % 2])
                    mk[("pdone", id(scr), c0, r0)] = markv(SP_[k % 2])
                    if r0 == (DC - 1) * 128 and scr is not wd_scr:
                        mk[("chunk", id(scr), c0)] = [markv("sp0"), markv("sp1")]
                    pstate["comp"] = k + 1

            for k in range(min(PF, n_prep)):
                prep_load(k)

            def chunk_key(scr, col):
                key_c0 = 0
                for (c0, w) in wchunks:
                    if c0 <= col < c0 + w:
                        key_c0 = c0
                return ("chunk", id(scr), key_c0)

            for b in range(NB):
                for tt in range(DC):
                    if (b, tt) != (0, 0):
                        prep_pump(1)
                    par = tt % 2
                    guard(y, ("xst", par))
                    y.dma_start(out=xst[par][:, :],
                                in_=x_h[b * T + tt * 128: b * T + (tt + 1) * 128, :]
                                ).inc(LX[par], 16)
                    mk[("xld", b, tt)] = markv(LX[par])
                    wmark(s, ("xld", b, tt))
                    guard(s, ("cst", par))
                    s.copy(cst[par][:, :], xst[par][:, :]).inc("aA", 1)
                    add_guard(("xst", par), "aA")
                    mk[("xcast", b, tt)] = markv("aA")
                    ptr = (PA if tt % 2 == 0 else PB)[:, :].bitcast(BF16)
                    pbanks = [0, 1] if tt % 2 == 0 else [2, 3]
                    wmark(t, ("xcast", b, tt))
                    pguard(t, pbanks)
                    for dc in range(DC):
                        t.transpose(ptr[:, dc * 128:(dc + 1) * 128],
                                    cst[par][:, dc * 128:(dc + 1) * 128],
                                    identb[:, :])
                    t.matmul(PD[0:1, 1008:1024], ones_r[0:1, 0:1],
                             ones_r[0:1, 0:16], start=True, stop=True).inc("tP", 1)
                    add_guard(("cst", par), "tP")
                    mk[("xtr", b, tt)] = markv("tP")
                    wmark(v, ("xtr", b, tt))
                    if b > 0 and tt == 0:
                        guard(v, ("arena_xfm",))
                    dst = arena[:, AR_XFM:AR_XFM + 32768].rearrange(
                        "p (dc t) -> p dc t", dc=DC)[:, :, tt * 128:(tt + 1) * 128]
                    v.tensor_copy(dst, ptr.rearrange("p (dc t) -> p dc t", dc=DC)).inc("vD", 1)
                    padd(pbanks, "vD")
                mk[("xfm", b)] = markv("vD")

                if b > 0:
                    wmark(g, ("attn_done", b - 1))
                g.dma_start(out=bqr[0:1, :], in_=bqkv_h[:, :]).inc("lbq", 16)
                mk[("bqr", b)] = markv("lbq")
                for ot in range(2 * H):
                    prep_pump(1)
                    slot = ot % 4
                    guard(g, ("wst", slot))
                    g.dma_start(
                        out=wst[:, slot * 2048:(slot + 1) * 2048].rearrange(
                            "p (dc o) -> p dc o", dc=DC),
                        in_=wqkv_h[:, ot * 128:(ot + 1) * 128].rearrange(
                            "(dc p) o -> p dc o", p=128)).inc(LW[slot], 16)
                    mk[("wqk", b, ot)] = markv(LW[slot])
                    wmark(t, ("wqk", b, ot))
                    wmark(t, ("bqr", b))
                    wmark(t, ("xfm", b))
                    for tc in range(4):
                        pi = (ot * 4 + tc) % 4
                        iq = ot * 4 + tc
                        pguard(t, [pi])
                        for dc in range(DC):
                            t.matmul(PS4[pi],
                                     wst[:, slot * 2048 + dc * 128: slot * 2048 + (dc + 1) * 128],
                                     arena[:, AR_XFM + dc * 2048 + tc * 512:
                                           AR_XFM + dc * 2048 + (tc + 1) * 512],
                                     start=(dc == 0), stop=False)
                        t.matmul(PS4[pi], bqr[0:1, ot * 128:(ot + 1) * 128],
                                 ones_r[0:1, 0:512], start=False, stop=True).inc("tP", 1)
                        mk[("qkmmc", b, iq)] = markv("tP")
                        if tc == 3:
                            add_guard(("wst", slot), "tP")
                        stg = arena[:, AR_QKST + pi * 512: AR_QKST + (pi + 1) * 512]
                        wmark(s, ("qkmmc", b, iq))
                        guard(s, ("qkst", pi))
                        s.copy(stg, PS4[pi]).inc("aA", 1)
                        padd([pi], "aA")
                        mk[("qkev", b, iq)] = markv("aA")
                        scr = q_scr if ot < H else k_scr
                        hh = ot % H
                        wmark(y, ("qkev", b, iq))
                        y.dma_start(out=scr[(b * H + hh) * 128:(b * H + hh + 1) * 128,
                                            tc * 512:(tc + 1) * 512],
                                    in_=stg).inc(SQ[pi], 16)
                        add_guard(("qkst", pi), SQ[pi])

                for oc in range(4):
                    prep_pump(2)
                    for sl in range(4):
                        guard(g, ("wst", sl))
                    g.dma_start(
                        out=wst[:, 0:8192].rearrange("p (dc o) -> p dc o", dc=DC),
                        in_=wqkv_h[:, 2 * D + oc * 512: 2 * D + (oc + 1) * 512].rearrange(
                            "(dc p) o -> p dc o", p=128)).inc("lw0", 16)
                    mk[("wv", b, oc)] = markv("lw0")
                    wmark(t, ("wv", b, oc))
                    for tt in range(DC):
                        pi = (oc * DC + tt) % 4
                        iq = oc * DC + tt
                        pguard(t, [pi])
                        for dc in range(DC):
                            t.matmul(PS4[pi],
                                     arena[:, AR_XFM + dc * 2048 + tt * 128:
                                           AR_XFM + dc * 2048 + (tt + 1) * 128],
                                     wst[:, dc * 512:(dc + 1) * 512],
                                     start=(dc == 0), stop=False)
                        t.matmul(PS4[pi], ones_r[0:1, 0:128],
                                 bqr[0:1, 2 * D + oc * 512: 2 * D + (oc + 1) * 512],
                                 start=False, stop=True).inc("tP", 1)
                        mk[("vmmc", b, iq)] = markv("tP")
                        if tt == DC - 1:
                            for sl in range(4):
                                add_guard(("wst", sl), "tP")
                        stg = arena[:, AR_QKST + pi * 512: AR_QKST + (pi + 1) * 512]
                        wmark(s, ("vmmc", b, iq))
                        guard(s, ("qkst", pi))
                        s.copy(stg, PS4[pi]).inc("aA", 1)
                        padd([pi], "aA")
                        mk[("vev", b, iq)] = markv("aA")
                        wmark(y, ("vev", b, iq))
                        y.dma_start(out=v_scr[b * T + tt * 128: b * T + (tt + 1) * 128,
                                              oc * 512:(oc + 1) * 512],
                                    in_=stg).inc(SQ[pi], 16)
                        add_guard(("qkst", pi), SQ[pi])
                add_guard(("arena_xfm",), "tP")
                mk[("qkvst", b)] = [markv(sq) for sq in SQ]

                guard(g, ("arena_xfm",))
                add_guard(("arena_xfm",), "tP")
                for dc in range(DC):
                    g.dma_start(out=arena[:, AR_XFM + dc * 2048: AR_XFM + (dc + 1) * 2048],
                                in_=wproj_h[dc * 128:(dc + 1) * 128, :]).inc("lwp", 16)
                mk[("wproj", b)] = markv("lwp")

                for h in range(H):
                    prep_pump(2)
                    hp = h % 2
                    guard(y, ("attld", hp))
                    wmark(y, ("qkvst", b))
                    y.dma_start(out=arena[:, AR_QH + hp * 2048: AR_QH + (hp + 1) * 2048],
                                in_=q_scr[(b * H + h) * 128:(b * H + h + 1) * 128, :]
                                ).inc(LT[hp], 16)
                    y.dma_start(out=arena[:, AR_KH + hp * 2048: AR_KH + (hp + 1) * 2048],
                                in_=k_scr[(b * H + h) * 128:(b * H + h + 1) * 128, :]
                                ).inc(LT[hp], 16)
                    y.dma_start(out=arena[:, AR_VH + hp * 2048: AR_VH + (hp + 1) * 2048
                                          ].rearrange("p (kt o) -> p kt o", kt=DC),
                                in_=v_scr[b * T:(b + 1) * T, h * 128:(h + 1) * 128].rearrange(
                                    "(kt p) o -> p kt o", p=128)).inc(LT[hp], 16)
                    mk[("attld", b, h)] = markv(LT[hp])
                    qh = arena[:, AR_QH + hp * 2048: AR_QH + (hp + 1) * 2048]
                    kh = arena[:, AR_KH + hp * 2048: AR_KH + (hp + 1) * 2048]
                    vh = arena[:, AR_VH + hp * 2048: AR_VH + (hp + 1) * 2048]
                    for half in range(2):
                        wmark(t, ("attld", b, h))
                        pguard(t, [4, 5])
                        pguard(t, [6, 7])
                        def emit_scores(kt):
                            Sp = PA if kt % 2 == 0 else PB
                            sbanks = [0, 1] if kt % 2 == 0 else [2, 3]
                            pguard(t, sbanks)
                            for qc in range(2):
                                t.matmul(Sp[:, qc * 512:(qc + 1) * 512],
                                         kh[:, kt * 128:(kt + 1) * 128],
                                         qh[:, half * 1024 + qc * 512:
                                            half * 1024 + (qc + 1) * 512],
                                         start=True, stop=True).inc("tP", 1)
                            mk[("score", b, h, half, kt)] = markv("tP")
                            pbt = arena[:, AR_PBT + (kt % 2) * 1024:
                                        AR_PBT + (kt % 2 + 1) * 1024]
                            wmark(s, ("score", b, h, half, kt))
                            guard(s, ("pbt", kt % 2))
                            s.activation(out=pbt, in_=Sp[:, 0:1024], func=AF.Exp,
                                         bias=c0col[:, 0:1], scale=SCALE).inc("aA", 1)
                            padd(sbanks, "aA")
                            mk[("exp", b, h, half, kt)] = markv("aA")

                        def emit_sumav(kt):
                            pbt = arena[:, AR_PBT + (kt % 2) * 1024:
                                        AR_PBT + (kt % 2 + 1) * 1024]
                            wmark(t, ("exp", b, h, half, kt))
                            for qc in range(2):
                                t.matmul(PD[0:1, qc * 512:(qc + 1) * 512], ones_c[:, 0:1],
                                         pbt[:, qc * 512:(qc + 1) * 512],
                                         start=(kt == 0), stop=(kt == DC - 1))
                            last = None
                            for qc in range(2):
                                last = t.matmul(PC[:, qc * 512:(qc + 1) * 512],
                                                vh[:, kt * 128:(kt + 1) * 128],
                                                pbt[:, qc * 512:(qc + 1) * 512],
                                                start=(kt == 0), stop=(kt == DC - 1))
                            last.inc("tP", 1)
                            add_guard(("pbt", kt % 2), "tP")
                            mk[("av", b, h, half, kt)] = markv("tP")

                        for kt in range(DC):
                            emit_scores(kt)
                            if kt >= 1:
                                emit_sumav(kt - 1)
                        emit_sumav(DC - 1)
                        if half == 1:
                            add_guard(("attld", hp), "tP")
                        mk[("avdone2", b, h, half)] = markv("tP")
                        wmark(s, ("avdone2", b, h, half))
                        guard(s, ("recipb",))
                        s.activation(out=recip[0:1, :], in_=PD[0:1, 0:1024],
                                     func=AF.Ln, scale=1.0).inc("aA", 1)
                        s.wait("aA", cnt["aA"])
                        s.activation(out=recipb[0:1, :], in_=recip[0:1, :],
                                     func=AF.Exp, scale=-1.0).inc("aA", 1)
                        padd([6, 7], "aA")
                        mk[("recipb", b, h, half)] = markv("aA")
                        wmark(t, ("recipb", b, h, half))
                        pguard(t, [0, 1])
                        last = None
                        for qc in range(2):
                            last = t.matmul(PA[:, qc * 512:(qc + 1) * 512],
                                            ones_r[0:1, 0:128],
                                            recipb[0:1, qc * 512:(qc + 1) * 512],
                                            start=True, stop=True)
                        last.inc("tP", 1)
                        add_guard(("recipb",), "tP")
                        mk[("rbc", b, h, half)] = markv("tP")
                        wmark(v, ("rbc", b, h, half))
                        v.tensor_copy(rcb[:, 0:1024], PA[:, 0:1024]).inc("vD", 1)
                        padd([0, 1], "vD")
                        v.wait("vD", cnt["vD"])
                        for qc in range(2):
                            stg = arena[:, AR_AVST + qc * 512: AR_AVST + (qc + 1) * 512]
                            guard(v, ("avst", qc))
                            v.tensor_tensor(out=stg, in0=PC[:, qc * 512:(qc + 1) * 512],
                                            in1=rcb[:, qc * 512:(qc + 1) * 512],
                                            op=ALU.mult).inc("vD", 1)
                            mk[("avev_c", b, h, half, qc)] = markv("vD")
                            if qc == 1:
                                padd([4, 5], "vD")
                            wmark(y, ("avev_c", b, h, half, qc))
                            y.dma_start(
                                out=ao_scr[(b * H + h) * 128:(b * H + h + 1) * 128,
                                           half * 1024 + qc * 512:
                                           half * 1024 + (qc + 1) * 512],
                                in_=stg).inc(SV[qc], 16)
                            add_guard(("avst", qc), SV[qc])
                mk[("ao_all", b)] = [markv(svn) for svn in SV]
                mk[("attn_done", b)] = markv("tP")

                wmark(g, ("attn_done", b))
                g.dma_start(out=bpr[0:1, :], in_=bproj_h[:, :]).inc("lbp", 16)
                mk[("bpr", b)] = markv("lbp")
                for tt in range(DC):
                    prep_pump(2)
                    par = tt % 2
                    guard(y, ("xst", par))
                    y.dma_start(out=xst[par][:, :],
                                in_=x_h[b * T + tt * 128: b * T + (tt + 1) * 128, :]
                                ).inc(LX[par], 16)
                    mk[("res", b, tt)] = markv(LX[par])
                    slot = tt % 4
                    guard(g, ("wst", slot))
                    wmark(g, ("ao_all", b))
                    g.dma_start(out=wst[:, slot * 2048:(slot + 1) * 2048].rearrange(
                                    "p (dc t) -> p dc t", dc=DC),
                                in_=ao_scr[b * DC * 128:(b + 1) * DC * 128,
                                           tt * 128:(tt + 1) * 128].rearrange(
                                    "(dc p) t -> p dc t", p=128)).inc(LW[slot], 16)
                    mk[("aold", b, tt)] = markv(LW[slot])
                    wmark(t, ("aold", b, tt))
                    wmark(t, ("wproj", b))
                    wmark(t, ("bpr", b))
                    for oc in range(4):
                        pi = (tt * 4 + oc) % 4
                        pguard(t, [pi])
                        for dc in range(DC):
                            t.matmul(PS4[pi],
                                     wst[:, slot * 2048 + dc * 128:
                                         slot * 2048 + (dc + 1) * 128],
                                     arena[:, AR_XFM + dc * 2048 + oc * 512:
                                           AR_XFM + dc * 2048 + (oc + 1) * 512],
                                     start=(dc == 0), stop=False)
                        t.matmul(PS4[pi], ones_r[0:1, 0:128],
                                 bpr[0:1, oc * 512:(oc + 1) * 512],
                                 start=False, stop=True).inc("tP", 1)
                        mk[("projc", b, tt, oc)] = markv("tP")
                        if oc == 3:
                            add_guard(("wst", slot), "tP")
                            add_guard(("arena_xfm",), "tP")
                        wmark(v, ("projc", b, tt, oc))
                        wmark(v, ("res", b, tt))
                        if oc == 0:
                            guard(v, ("pst", par))
                        v.tensor_tensor(out=pst[par][:, oc * 512:(oc + 1) * 512],
                                        in0=PS4[pi],
                                        in1=xst[par][:, oc * 512:(oc + 1) * 512],
                                        op=ALU.add).inc("vD", 1)
                        if oc == 3:
                            add_guard(("xst", par), "vD")
                        padd([pi], "vD")
                        mk[("projadd", b, tt, oc)] = markv("vD")
                    guard(v, ("cst", par))
                    v.wait("vD", cnt["vD"])
                    v.tensor_scalar(out=cst[par][:, :], in0=pst[par][:, :],
                                    scalar1=RD, scalar2=None, op0=ALU.mult,
                                    op1=ALU.add, accum_out=mean[:, 0:1]).inc("vD", 1)
                    mk[("lnmean", b, tt)] = markv("vD")
                    v.wait("vD", cnt["vD"])
                    wmark(s, ("lnmean", b, tt))
                    guard(s, ("trst",))
                    s.activation(out=arena[:, AR_TRST:AR_TRST + 2048],
                                 in_=pst[par][:, :], func=AF.Square, scale=RSD,
                                 accum_out=ssq[:, 0:1]).inc("aA", 1)
                    mk[("lnssq", b, tt)] = markv("aA")
                    wmark(v, ("lnssq", b, tt))
                    v.scalar_tensor_tensor(out=var[:, 0:1], in0=mean[:, 0:1],
                                           scalar=mean[:, 0:1], in1=ssq[:, 0:1],
                                           op0=ALU.mult, op1=ALU.subtract).inc("vD", 1)
                    mk[("lnvar", b, tt)] = markv("vD")
                    wmark(s, ("lnvar", b, tt))
                    s.activation(out=std[:, 0:1], in_=var[:, 0:1], func=AF.Sqrt,
                                 bias=epscol[:, 0:1], scale=-1.0).inc("aA", 1)
                    mk[("lnstd", b, tt)] = markv("aA")
                    wmark(v, ("lnstd", b, tt))
                    v.reciprocal(rstd[:, 0:1], std[:, 0:1]).inc("vD", 1)
                    v.wait("vD", cnt["vD"])
                    v.tensor_scalar(out=pst[par][:, :], in0=pst[par][:, :],
                                    scalar1=mean[:, 0:1], scalar2=rstd[:, 0:1],
                                    op0=ALU.subtract, op1=ALU.mult).inc("vD", 1)
                    mk[("x1", b, tt)] = markv("vD")
                    wmark(y, ("x1", b, tt))
                    y.dma_start(out=x1tm_scr[b * T + tt * 128: b * T + (tt + 1) * 128, :],
                                in_=pst[par][:, :]).inc(SM[par], 16)
                    add_guard(("pst", par), SM[par])
                    wmark(s, ("x1", b, tt))
                    guard(s, ("cst", par))
                    s.copy(cst[par][:, :], pst[par][:, :]).inc("aA", 1)
                    mk[("x1bf", b, tt)] = markv("aA")
                    guard(s, ("trst",))
                    for dc in range(DC):
                        s.dma_start_transpose(
                            out=arena[:, AR_TRST + dc * 128: AR_TRST + (dc + 1) * 128],
                            in_=cst[par][:, dc * 128:(dc + 1) * 128]).inc("sxt", 16)
                    add_guard(("cst", par), "sxt")
                    mk[("x1trd", b, tt)] = markv("sxt")
                    wmark(s, ("x1trd", b, tt))
                    s.dma_start(out=x1fm_scr[b * DC * 128:(b + 1) * DC * 128,
                                             tt * 128:(tt + 1) * 128].rearrange(
                                    "(dc p) t -> p dc t", p=128),
                                in_=arena[:, AR_TRST:AR_TRST + 2048].rearrange(
                                    "p (dc t) -> p dc t", dc=DC)).inc("sxf", 16)
                    add_guard(("trst",), "sxf")
                mk[("x1_all", b)] = markv("sxf")
                mk[("x1tm_all", b)] = [markv(smn) for smn in SM]
                mk[("projmm_done", b)] = markv("tP")

            for b in range(NB):
                for ch in range(2):
                    cidx = b * 2 + ch
                    guard(y, ("ar_x1fm",))
                    wmark(y, ("x1_all", NB - 1))
                    y.dma_start(out=arena[:, AR_X1FM:AR_X1FM + 16384].rearrange(
                                    "p (dc t) -> p dc t", dc=DC),
                                in_=x1fm_scr[b * DC * 128:(b + 1) * DC * 128,
                                             ch * 1024:(ch + 1) * 1024].rearrange(
                                    "(dc p) t -> p dc t", p=128)).inc("lxf", 16)
                    mk[("x1fmld", cidx)] = markv("lxf")

                    for it in range(NIT):
                        if pstate["comp"] < n_prep:
                            prep_pump(1)
                        wgslot = it % 2
                        wuslot = 2 + it % 2
                        guard(g, ("wst", wgslot))
                        wmark(g, chunk_key(wg_scr, it * 128))
                        g.dma_start(out=wst[:, wgslot * 2048:(wgslot + 1) * 2048].rearrange(
                                        "p (hc i) -> p hc i", hc=DC),
                                    in_=wg_scr[:, it * 128:(it + 1) * 128].rearrange(
                                        "(hc p) i -> p hc i", p=128)).inc(LW[wgslot], 16)
                        mk[("wgld", cidx, it)] = markv(LW[wgslot])
                        guard(g, ("wst", wuslot))
                        wmark(g, chunk_key(wu_scr, it * 128))
                        g.dma_start(out=wst[:, wuslot * 2048:(wuslot + 1) * 2048].rearrange(
                                        "p (hc i) -> p hc i", hc=DC),
                                    in_=wu_scr[:, it * 128:(it + 1) * 128].rearrange(
                                        "(hc p) i -> p hc i", p=128)).inc(LW[wuslot], 16)
                        mk[("wuld", cidx, it)] = markv(LW[wuslot])
                        Gp, gbanks = (PA, [0, 1]) if it % 2 == 0 else (PC, [4, 5])
                        Up, ubanks = (PB, [2, 3]) if it % 2 == 0 else (PD, [6, 7])
                        wmark(t, ("wgld", cidx, it))
                        wmark(t, ("x1fmld", cidx))
                        pguard(t, gbanks)
                        last = None
                        for tc2 in range(2):
                            for hc in range(DC):
                                last = t.matmul(
                                    Gp[:, tc2 * 512:(tc2 + 1) * 512],
                                    wst[:, wgslot * 2048 + hc * 128:
                                        wgslot * 2048 + (hc + 1) * 128],
                                    arena[:, AR_X1FM + hc * 1024 + tc2 * 512:
                                          AR_X1FM + hc * 1024 + (tc2 + 1) * 512],
                                    start=(hc == 0), stop=(hc == DC - 1))
                        last.inc("tP", 1)
                        add_guard(("wst", wgslot), "tP")
                        mk[("gmm", cidx, it)] = markv("tP")
                        wmark(t, ("wuld", cidx, it))
                        pguard(t, ubanks)
                        for tc2 in range(2):
                            for hc in range(DC):
                                last = t.matmul(
                                    Up[:, tc2 * 512:(tc2 + 1) * 512],
                                    wst[:, wuslot * 2048 + hc * 128:
                                        wuslot * 2048 + (hc + 1) * 128],
                                    arena[:, AR_X1FM + hc * 1024 + tc2 * 512:
                                          AR_X1FM + hc * 1024 + (tc2 + 1) * 512],
                                    start=(hc == 0), stop=(hc == DC - 1))
                        last.inc("tP", 1)
                        add_guard(("wst", wuslot), "tP")
                        if it == NIT - 1:
                            add_guard(("ar_x1fm",), "tP")
                        mk[("umm", cidx, it)] = markv("tP")
                        gbs = arena[:, AR_GB + (it % 2) * 1024: AR_GB + (it % 2 + 1) * 1024]
                        wmark(s, ("gmm", cidx, it))
                        guard(s, ("gb", it % 2))
                        s.activation(out=gbs, in_=Gp[:, 0:1024], func=AF.Silu,
                                     bias=0.0, scale=1.0).inc("aA", 1)
                        padd(gbanks, "aA")
                        mk[("silu", cidx, it)] = markv("aA")
                        wmark(v, ("umm", cidx, it))
                        wmark(v, ("silu", cidx, it))
                        if it == 0:
                            guard(v, ("ar_h",))
                        v.tensor_tensor(out=arena[:, AR_H + it * 1024: AR_H + (it + 1) * 1024],
                                        in0=Up[:, 0:1024], in1=gbs, op=ALU.mult).inc("vD", 1)
                        padd(ubanks, "vD")
                        add_guard(("gb", it % 2), "vD")
                        mk[("hmul", cidx, it)] = markv("vD")

                    for oc in range(4):
                        for ic in range(NIT):
                            wp = ic % 2
                            guard(g, ("wdst", wp))
                            if cidx == 0 and oc == 0:
                                wmark(g, ("pdone", id(wd_scr), 0, ic * 128))
                            g.dma_start(out=wdst[wp][:, :],
                                        in_=wd_scr[ic * 128:(ic + 1) * 128,
                                                   oc * 512:(oc + 1) * 512]).inc(LD_[wp], 16)
                            mk[("wdld", cidx, oc, ic)] = markv(LD_[wp])
                            wmark(t, ("wdld", cidx, oc, ic))
                            if ic == 0:
                                wmark(t, ("hmul", cidx, NIT - 1))
                                for tt2 in range(8):
                                    pguard(t, [tt2])
                            last = None
                            for tt2 in range(8):
                                last = t.matmul(
                                    PS8[tt2],
                                    arena[:, AR_H + ic * 1024 + tt2 * 128:
                                          AR_H + ic * 1024 + (tt2 + 1) * 128],
                                    wdst[wp][:, :],
                                    start=(ic == 0), stop=(ic == NIT - 1))
                            last.inc("tP", 1)
                            add_guard(("wdst", wp), "tP")
                        mk[("dmmoc", cidx, oc)] = markv("tP")
                        if oc == 3:
                            add_guard(("ar_h",), "tP")
                        for tt2 in range(8):
                            gtile = b * T + ch * 1024 + tt2 * 128
                            par = tt2 % 2
                            guard(y, ("xst", par))
                            wmark(y, ("x1tm_all", b))
                            y.dma_start(out=xst[par][:, oc * 512:(oc + 1) * 512],
                                        in_=x1tm_scr[gtile:gtile + 128,
                                                     oc * 512:(oc + 1) * 512]
                                        ).inc(LX[par], 16)
                            mk[("x1res", cidx, oc, tt2)] = markv(LX[par])
                            si = par * 4 + ((tt2 // 2) % 2) * 2 + oc % 2
                            stg = pst[par][:, ((tt2 // 2) % 2) * 1024 + (oc % 2) * 512:
                                           ((tt2 // 2) % 2) * 1024 + (oc % 2 + 1) * 512]
                            wmark(v, ("dmmoc", cidx, oc))
                            wmark(v, ("x1res", cidx, oc, tt2))
                            if cidx == 0 and oc == 0:
                                wmark(v, ("x1tm_all", NB - 1))
                            guard(v, ("stg", si))
                            guard(v, ("cst", par))
                            v.tensor_tensor(out=stg, in0=PS8[tt2],
                                            in1=xst[par][:, oc * 512:(oc + 1) * 512],
                                            op=ALU.add).inc("vD", 1)
                            v.wait("vD", cnt["vD"])
                            v.tensor_scalar(out=cst[par][:, 0:512], in0=stg,
                                            scalar1=RD, scalar2=None, op0=ALU.mult,
                                            op1=ALU.add,
                                            accum_out=pmean[:, tt2 * 4 + oc:
                                                            tt2 * 4 + oc + 1]).inc("vD", 1)
                            padd([tt2], "vD")
                            add_guard(("xst", par), "vD")
                            mk[("dadd", cidx, oc, tt2)] = markv("vD")
                            wmark(s, ("dadd", cidx, oc, tt2))
                            s.activation(out=cst[par][:, 512:1024], in_=stg,
                                         func=AF.Square, scale=RSD,
                                         accum_out=pssq[:, tt2 * 4 + oc:
                                                        tt2 * 4 + oc + 1]).inc("aA", 1)
                            mk[("dsq", cidx, oc, tt2)] = markv("aA")
                            s.dma_start(out=x2_scr[gtile:gtile + 128,
                                                   oc * 512:(oc + 1) * 512],
                                        in_=stg).inc(SG[si], 16)
                            add_guard(("stg", si), SG[si])
                            mk[("x2st", cidx, oc, tt2)] = markv(SG[si])

                    for tt2 in range(8):
                        gtile = b * T + ch * 1024 + tt2 * 128
                        par = tt2 % 2
                        guard(y, ("xst", par))
                        for oc in range(4):
                            wmark(y, ("x2st", cidx, oc, tt2))
                        y.dma_start(out=xst[par][:, :],
                                    in_=x2_scr[gtile:gtile + 128, :]).inc(LX[par], 16)
                        mk[("x2ld", cidx, tt2)] = markv(LX[par])
                        wmark(v, ("x2ld", cidx, tt2))
                        wmark(v, ("dsq", cidx, 3, tt2))
                        v.wait("vD", cnt["vD"])
                        v.tensor_reduce(mean[:, 0:1], pmean[:, tt2 * 4:(tt2 + 1) * 4],
                                        axis=mybir.AxisListType.X, op=ALU.add)
                        v.tensor_reduce(ssq[:, 0:1], pssq[:, tt2 * 4:(tt2 + 1) * 4],
                                        axis=mybir.AxisListType.X, op=ALU.add).inc("vD", 1)
                        v.wait("vD", cnt["vD"])
                        v.scalar_tensor_tensor(out=var[:, 0:1], in0=mean[:, 0:1],
                                               scalar=mean[:, 0:1], in1=ssq[:, 0:1],
                                               op0=ALU.mult, op1=ALU.subtract).inc("vD", 1)
                        mk[("fvar", cidx, tt2)] = markv("vD")
                        wmark(s, ("fvar", cidx, tt2))
                        s.activation(out=std[:, 0:1], in_=var[:, 0:1], func=AF.Sqrt,
                                     bias=epscol[:, 0:1], scale=-1.0).inc("aA", 1)
                        mk[("fstd", cidx, tt2)] = markv("aA")
                        wmark(v, ("fstd", cidx, tt2))
                        v.reciprocal(rstd[:, 0:1], std[:, 0:1]).inc("vD", 1)
                        v.wait("vD", cnt["vD"])
                        v.tensor_scalar(out=xst[par][:, :], in0=xst[par][:, :],
                                        scalar1=mean[:, 0:1], scalar2=rstd[:, 0:1],
                                        op0=ALU.subtract, op1=ALU.mult).inc("vD", 1)
                        mk[("outn", cidx, tt2)] = markv("vD")
                        wmark(y, ("outn", cidx, tt2))
                        y.dma_start(out=out_h[gtile:gtile + 128, :],
                                    in_=xst[par][:, :]).inc(SO[par], 16)
                        add_guard(("xst", par), SO[par])

            assert pstate["comp"] == n_prep, (pstate["comp"], n_prep)
            for semn in DMA_SEMS:
                y.wait(semn, cnt[semn])

        @block.tensor
        def _(eng):
            walk(_Walk(semh, "t", eng))

        @block.vector
        def _(eng):
            walk(_Walk(semh, "v", eng))

        @block.scalar
        def _(eng):
            walk(_Walk(semh, "s", eng))

        @block.gpsimd
        def _(eng):
            walk(_Walk(semh, "g", eng))

        @block.sync
        def _(eng):
            walk(_Walk(semh, "y", eng))

    return nc


def kernel(L0_attn_x, L0_attn_w_qkv, L0_attn_b_qkv, L0_attn_w_proj, L0_attn_b_proj,
           L0_ffn_lut, L0_ffn_gate_idx, L0_ffn_up_idx, L0_ffn_down_idx):
    x = np.ascontiguousarray(np.asarray(L0_attn_x, dtype=np.float32))
    w_qkv = np.ascontiguousarray(np.asarray(L0_attn_w_qkv, dtype=np.float32))
    b_qkv = np.ascontiguousarray(np.asarray(L0_attn_b_qkv, dtype=np.float32).reshape(1, 3 * D))
    w_proj = np.ascontiguousarray(np.asarray(L0_attn_w_proj, dtype=np.float32))
    b_proj = np.ascontiguousarray(np.asarray(L0_attn_b_proj, dtype=np.float32).reshape(1, D))
    lut = np.ascontiguousarray(np.asarray(L0_ffn_lut, dtype=np.float32).reshape(1, 16))
    gidxT = np.ascontiguousarray(np.asarray(L0_ffn_gate_idx, dtype=np.int32).T)
    uidxT = np.ascontiguousarray(np.asarray(L0_ffn_up_idx, dtype=np.int32).T)
    didxT = np.ascontiguousarray(np.asarray(L0_ffn_down_idx, dtype=np.int32).T)
    ident = np.eye(128, dtype=np.float32)

    nc = _build_nc()
    in_maps = []
    for c in range(8):
        xs = np.ascontiguousarray(x[2 * c:2 * c + 2].reshape(NB * T, D))
        in_maps.append({
            "x": xs, "w_qkv": w_qkv, "b_qkv": b_qkv, "w_proj": w_proj,
            "b_proj": b_proj, "lut": lut, "gidxT": gidxT, "uidxT": uidxT,
            "didxT": didxT, "ident": ident,
        })
    res = run_bass_kernel_spmd(nc, in_maps, core_ids=list(range(8)))
    out = np.concatenate([r["out"].reshape(NB, T, D) for r in res.results], axis=0)
    return out


# version 03 (final)
# speedup vs baseline: 1.3269x; 1.1150x over previous
from contextlib import ExitStack

import numpy as np

import concourse.bass as bass
import concourse.mybir as mybir
from concourse.bass_utils import run_bass_kernel_spmd

F32 = mybir.dt.float32
BF16 = mybir.dt.bfloat16
I32 = mybir.dt.int32
AF = mybir.ActivationFunctionType
ALU = mybir.AluOpType

D = 2048
DC = 16
T = 2048
NB = 2
H = 16
HD = 128
ITER = 5504
NIT = 43
SCALE = 1.0 / float(np.sqrt(HD))
C0 = 12.0
EPS = 1e-5
RD = 1.0 / D
RSD = 1.0 / float(np.sqrt(D))

AR_XFM = 0
AR_QKST = 32768
AR_QH = 34816
AR_KH = 38912
AR_VH = 43008
AR_PBT = 47104
AR_AVST = 49152
AR_RC = 50176
AR_RB = 51200
AR_TRST = 52224
AR_BPR = 54272
AR_H = 0
AR_X1FM = 44032
AR_GB = 60416
AR_COLS = 62464

PF = 3

DMA_SEMS = (
    ["lxa", "lxb", "lw0", "lw1", "lw2", "lw3", "ld0", "ld1", "lp0", "lp1", "lp2",
     "lt0", "lt1", "lxf", "lwp", "lbq", "lbp", "lcs"]
    + ["sq0", "sq1", "sq2", "sq3", "sv0", "sv1", "sm0", "sm1"]
    + [f"sg{i}" for i in range(8)]
    + ["sxf", "sp0", "sp1", "so0", "so1", "sxa", "sxt"]
)
ALL_SEMS = DMA_SEMS + ["tP", "vD", "aA", "gP", "ccs"]
LX = ["lxa", "lxb"]
LW = ["lw0", "lw1", "lw2", "lw3"]
LD_ = ["ld0", "ld1"]
LP = ["lp0", "lp1", "lp2"]
LT = ["lt0", "lt1"]
SQ = ["sq0", "sq1", "sq2", "sq3"]
SV = ["sv0", "sv1"]
SM = ["sm0", "sm1"]
SG = [f"sg{i}" for i in range(8)]
SP_ = ["sp0", "sp1"]
SO = ["so0", "so1"]


class _Instr:
    __slots__ = ("p", "real")

    def __init__(self, p, real):
        self.p = p
        self.real = real

    def inc(self, sem, k=1):
        self.p.cnt[sem] += k
        if self.real is not None:
            self.real.then_inc(self.p.semh[sem], k)
        return self


class _Eng:
    def __init__(self, p, name):
        self.p = p
        self.name = name

    def _real(self):
        return self.p.real_eng if self.p.emit_name == self.name else None

    def wait(self, sem, val):
        assert 0 <= val <= self.p.cnt[sem], (self.name, sem, val, self.p.cnt[sem])
        r = self._real()
        if r is not None and val > 0:
            r.wait_ge(self.p.semh[sem], val)

    def __getattr__(self, attr):
        r = self._real()
        if r is None:
            return lambda *a, **kw: _Instr(self.p, None)
        fn = getattr(r, attr)

        def wrap(*a, **kw):
            return _Instr(self.p, fn(*a, **kw))

        return wrap


class _Walk:
    def __init__(self, semh, emit_name, real_eng):
        self.semh = semh
        self.emit_name = emit_name
        self.real_eng = real_eng
        self.cnt = {k: 0 for k in semh}
        self.t = _Eng(self, "t")
        self.v = _Eng(self, "v")
        self.s = _Eng(self, "s")
        self.g = _Eng(self, "g")
        self.y = _Eng(self, "y")


def _build_nc():
    nc = bass.Bass("TRN2", target_bir_lowering=False, detect_race_conditions=False)

    x_h = nc.dram_tensor("x", [NB * T, D], F32, kind="ExternalInput")
    wqkv_h = nc.dram_tensor("w_qkv", [D, 3 * D], F32, kind="ExternalInput")
    bqkv_h = nc.dram_tensor("b_qkv", [1, 3 * D], F32, kind="ExternalInput")
    wproj_h = nc.dram_tensor("w_proj", [D, D], F32, kind="ExternalInput")
    bproj_h = nc.dram_tensor("b_proj", [1, D], F32, kind="ExternalInput")
    lut_h = nc.dram_tensor("lut", [1, 16], F32, kind="ExternalInput")
    gidxT_h = nc.dram_tensor("gidxT", [D // 8, ITER], I32, kind="ExternalInput")
    uidxT_h = nc.dram_tensor("uidxT", [D // 8, ITER], I32, kind="ExternalInput")
    didxT_h = nc.dram_tensor("didxT", [ITER // 8, D], I32, kind="ExternalInput")
    ident_h = nc.dram_tensor("ident", [128, 128], F32, kind="ExternalInput")
    out_h = nc.dram_tensor("out", [NB * T, D], F32, kind="ExternalOutput")

    q_scr = nc.dram_tensor("q_scr", [NB * H * HD, T], BF16, kind="Internal")
    k_scr = nc.dram_tensor("k_scr", [NB * H * HD, T], BF16, kind="Internal")
    v_scr = nc.dram_tensor("v_scr", [NB * T, D], BF16, kind="Internal")
    ao_scr = nc.dram_tensor("ao_scr", [NB * DC * 128, T], BF16, kind="Internal")
    x1tm_scr = nc.dram_tensor("x1tm_scr", [NB * T, D], F32, kind="Internal")
    x1fm_scr = nc.dram_tensor("x1fm_scr", [NB * DC * 128, T], BF16, kind="Internal")
    x2_scr = nc.dram_tensor("x2_scr", [NB * T, D], F32, kind="Internal")
    wg_src = nc.dram_tensor("wg_src", [D // 8, ITER], BF16, kind="Internal")
    wu_src = nc.dram_tensor("wu_src", [D // 8, ITER], BF16, kind="Internal")
    wd_src = nc.dram_tensor("wd_src", [ITER // 8, D], BF16, kind="Internal")
    wg_scr = nc.dram_tensor("wg_scr", [DC * 128, ITER], BF16, kind="Internal",
                            addr_space="Shared")
    wu_scr = nc.dram_tensor("wu_scr", [DC * 128, ITER], BF16, kind="Internal",
                            addr_space="Shared")
    wd_scr = nc.dram_tensor("wd_scr", [ITER, D], BF16, kind="Internal",
                            addr_space="Shared")

    with ExitStack() as ctx:
        e = ctx.enter_context
        arena = e(nc.sbuf_tensor("arena", [128, AR_COLS], BF16))
        xst = [e(nc.sbuf_tensor(f"xst{i}", [128, D], F32)) for i in range(2)]
        pst = [e(nc.sbuf_tensor(f"pst{i}", [128, D], F32)) for i in range(2)]
        cst = [e(nc.sbuf_tensor(f"cst{i}", [128, D], BF16)) for i in range(2)]
        wst = e(nc.sbuf_tensor("wst", [128, 8192], BF16))
        wdst = [e(nc.sbuf_tensor(f"wdst{i}", [128, 512], BF16)) for i in range(2)]
        pidx = [e(nc.sbuf_tensor(f"pidx{i}", [128, D], BF16)) for i in range(PF)]
        pacc = [e(nc.sbuf_tensor(f"pacc{i}", [128, D], BF16)) for i in range(2)]
        ptmp = e(nc.sbuf_tensor("ptmp", [128, D], BF16))
        identb = e(nc.sbuf_tensor("identb", [128, 128], BF16))
        ones_r = e(nc.sbuf_tensor("ones_r", [1, 512], BF16))
        ones_c = e(nc.sbuf_tensor("ones_c", [128, 1], BF16))
        luts = e(nc.sbuf_tensor("luts", [1, 16], BF16))
        lutb = e(nc.sbuf_tensor("lutb", [128, 16], F32))
        stat = [e(nc.sbuf_tensor(f"stat{i}", [128, 1], F32)) for i in range(6)]
        c0col = e(nc.sbuf_tensor("c0col", [128, 1], F32))
        epscol = e(nc.sbuf_tensor("epscol", [128, 1], F32))
        pmean = e(nc.sbuf_tensor("pmean", [128, 32], F32))
        pssq = e(nc.sbuf_tensor("pssq", [128, 32], F32))
        PA = e(nc.psum_tensor("PA", [128, 1024], F32))
        PB = e(nc.psum_tensor("PB", [128, 1024], F32))
        PC = e(nc.psum_tensor("PC", [128, 1024], F32))
        PD = e(nc.psum_tensor("PD", [128, 1024], F32))
        semh = {}
        for sname in ALL_SEMS:
            semh[sname] = e(nc.semaphore(sname))
        block = e(nc.Block())

        bqr = arena[0:1, AR_QH:AR_QH + 3 * D]
        bpr = arena[0:1, AR_BPR:AR_BPR + D]
        recip = arena[0:1, AR_QKST:AR_QKST + 2048].bitcast(F32)
        recipb = arena[0:1, AR_RB:AR_RB + 1024]
        rcb = arena[:, AR_RC:AR_RC + 1024]
        PS4 = [PA[:, 0:512], PA[:, 512:1024], PB[:, 0:512], PB[:, 512:1024]]
        PS8 = PS4 + [PC[:, 0:512], PC[:, 512:1024], PD[:, 0:512], PD[:, 512:1024]]
        mean, ssq, var, std, rstd, mr = stat

        def walk(P: _Walk):
            t, v, s, g, y = P.t, P.v, P.s, P.g, P.y
            cnt = P.cnt
            mk = {}
            guards = {}

            def markv(semn):
                return (semn, cnt[semn])

            def wmark(eng, key):
                mv = mk[key]
                if isinstance(mv, list):
                    for semn, vv in mv:
                        eng.wait(semn, vv)
                else:
                    eng.wait(mv[0], mv[1])

            def add_guard(key, semn):
                guards.setdefault(key, []).append((semn, cnt[semn]))

            def guard(eng, key):
                for semn, valv in guards.pop(key, ()):
                    eng.wait(semn, valv)

            def pguard(eng, banks):
                for bk in banks:
                    guard(eng, ("pb", bk))

            def padd(banks, semn):
                for bk in banks:
                    add_guard(("pb", bk), semn)

            g.memset(ones_r[:, :], 1.0)
            g.memset(c0col[:, :], -C0)
            g.memset(epscol[:, :], EPS)
            g.memset(ones_c[:, :], 1.0).inc("gP", 1)
            g.dma_start(out=identb[:, :], in_=ident_h[:, :]).inc("lcs", 16)
            g.dma_start(out=luts[:, :], in_=lut_h[:, :]).inc("lcs", 16)
            t.wait("lcs", 32)
            t.wait("gP", 1)
            s.wait("gP", 1)
            t.matmul(PA[:, 0:16], ones_r[0:1, 0:128], luts[0:1, :],
                     start=True, stop=True).inc("tP", 1)
            mk["lutbc"] = markv("tP")
            wmark(v, "lutbc")
            v.tensor_copy(lutb[:, :], PA[:, 0:16]).inc("vD", 1)
            mk["lutb"] = markv("vD")
            v.wait("vD", cnt["vD"])
            padd([0], "vD")

            plan = []
            wchunks = [(0, 2048), (2048, 2048), (4096, 1408)]
            for scr, src in ((wg_src, gidxT_h), (wu_src, uidxT_h)):
                for (c0, w) in wchunks:
                    for rt in range(D // 8 // 128):
                        plan.append((scr, rt * 128, src, rt * 128, c0, w, 128))
            for it in range((ITER // 8 + 127) // 128):
                rows = min(128, ITER // 8 - it * 128)
                plan.append((wd_src, it * 128, didxT_h, it * 128, 0, 2048, rows))
            n_prep = len(plan)
            pstate = {"comp": 0}

            def prep_load(k):
                scr, r0, src, sr0, c0, w, rows = plan[k]
                if k >= PF:
                    wmark(g, ("ptile", k - PF))
                g.dma_start(out=pidx[k % PF][0:rows, 0:w],
                            in_=src[sr0:sr0 + rows, c0:c0 + w]).inc(LP[k % PF], 16)
                mk[("pload", k)] = markv(LP[k % PF])

            def prep_pump(n):
                for _ in range(n):
                    k = pstate["comp"]
                    if k >= n_prep:
                        return
                    scr, r0, src, sr0, c0, w, rows = plan[k]
                    acc = pacc[k % 2]
                    wmark(v, ("pload", k))
                    guard(v, ("pacc", k % 2))
                    src_t = pidx[k % PF]
                    last = None
                    for j in range(16):
                        if j == 0:
                            last = v.tensor_scalar(
                                out=acc[0:rows, 0:w], in0=src_t[0:rows, 0:w],
                                scalar1=0.0, scalar2=lutb[0:rows, 0:1],
                                op0=ALU.is_equal, op1=ALU.mult)
                        else:
                            v.tensor_scalar(
                                out=ptmp[0:rows, 0:w], in0=src_t[0:rows, 0:w],
                                scalar1=float(j), scalar2=lutb[0:rows, j:j + 1],
                                op0=ALU.is_equal, op1=ALU.mult)
                            last = v.tensor_tensor(
                                out=acc[0:rows, 0:w], in0=acc[0:rows, 0:w],
                                in1=ptmp[0:rows, 0:w], op=ALU.add)
                    last.inc("vD", 1)
                    mk[("ptile", k)] = markv("vD")
                    if k + PF < n_prep:
                        prep_load(k + PF)
                    wmark(g, ("ptile", k))
                    g.dma_start(out=scr[r0:r0 + rows, c0:c0 + w],
                                in_=acc[0:rows, 0:w]).inc(SP_[k % 2], 16)
                    add_guard(("pacc", k % 2), SP_[k % 2])
                    pstate["comp"] = k + 1
                    if pstate["comp"] == n_prep:
                        mk[("prepall",)] = [markv("sp0"), markv("sp1")]

            for k in range(min(PF, n_prep)):
                prep_load(k)

            for b in range(NB):
                for tt in range(DC):
                    if (b, tt) != (0, 0):
                        prep_pump(1)
                    par = tt % 2
                    guard(y, ("xst", par))
                    y.dma_start(out=xst[par][:, :],
                                in_=x_h[b * T + tt * 128: b * T + (tt + 1) * 128, :]
                                ).inc(LX[par], 16)
                    mk[("xld", b, tt)] = markv(LX[par])
                    wmark(s, ("xld", b, tt))
                    guard(s, ("cst", par))
                    s.copy(cst[par][:, :], xst[par][:, :]).inc("aA", 1)
                    add_guard(("xst", par), "aA")
                    mk[("xcast", b, tt)] = markv("aA")
                    ptr = (PA if tt % 2 == 0 else PB)[:, :].bitcast(BF16)
                    pbanks = [0, 1] if tt % 2 == 0 else [2, 3]
                    wmark(t, ("xcast", b, tt))
                    pguard(t, pbanks)
                    for dc in range(DC):
                        t.transpose(ptr[:, dc * 128:(dc + 1) * 128],
                                    cst[par][:, dc * 128:(dc + 1) * 128],
                                    identb[:, :])
                    t.matmul(PD[0:1, 1008:1024], ones_r[0:1, 0:1],
                             ones_r[0:1, 0:16], start=True, stop=True).inc("tP", 1)
                    add_guard(("cst", par), "tP")
                    mk[("xtr", b, tt)] = markv("tP")
                    wmark(v, ("xtr", b, tt))
                    if b > 0 and tt == 0:
                        guard(v, ("arena_xfm",))
                    dst = arena[:, AR_XFM:AR_XFM + 32768].rearrange(
                        "p (dc t) -> p dc t", dc=DC)[:, :, tt * 128:(tt + 1) * 128]
                    v.tensor_copy(dst, ptr.rearrange("p (dc t) -> p dc t", dc=DC)).inc("vD", 1)
                    padd(pbanks, "vD")
                mk[("xfm", b)] = markv("vD")

                if b > 0:
                    wmark(g, ("attn_done", b - 1))
                g.dma_start(out=bqr[0:1, :], in_=bqkv_h[:, :]).inc("lbq", 16)
                mk[("bqr", b)] = markv("lbq")
                for ot in range(2 * H):
                    prep_pump(1)
                    slot = ot % 4
                    guard(g, ("wst", slot))
                    g.dma_start(
                        out=wst[:, slot * 2048:(slot + 1) * 2048].rearrange(
                            "p (dc o) -> p dc o", dc=DC),
                        in_=wqkv_h[:, ot * 128:(ot + 1) * 128].rearrange(
                            "(dc p) o -> p dc o", p=128)).inc(LW[slot], 16)
                    mk[("wqk", b, ot)] = markv(LW[slot])
                    wmark(t, ("wqk", b, ot))
                    wmark(t, ("bqr", b))
                    wmark(t, ("xfm", b))
                    for tc in range(4):
                        pi = (ot * 4 + tc) % 4
                        iq = ot * 4 + tc
                        pguard(t, [pi])
                        for dc in range(DC):
                            t.matmul(PS4[pi],
                                     wst[:, slot * 2048 + dc * 128: slot * 2048 + (dc + 1) * 128],
                                     arena[:, AR_XFM + dc * 2048 + tc * 512:
                                           AR_XFM + dc * 2048 + (tc + 1) * 512],
                                     start=(dc == 0), stop=False)
                        t.matmul(PS4[pi], bqr[0:1, ot * 128:(ot + 1) * 128],
                                 ones_r[0:1, 0:512], start=False, stop=True).inc("tP", 1)
                        mk[("qkmmc", b, iq)] = markv("tP")
                        if tc == 3:
                            add_guard(("wst", slot), "tP")
                        stg = arena[:, AR_QKST + pi * 512: AR_QKST + (pi + 1) * 512]
                        wmark(s, ("qkmmc", b, iq))
                        guard(s, ("qkst", pi))
                        s.copy(stg, PS4[pi]).inc("aA", 1)
                        padd([pi], "aA")
                        mk[("qkev", b, iq)] = markv("aA")
                        scr = q_scr if ot < H else k_scr
                        hh = ot % H
                        wmark(y, ("qkev", b, iq))
                        y.dma_start(out=scr[(b * H + hh) * 128:(b * H + hh + 1) * 128,
                                            tc * 512:(tc + 1) * 512],
                                    in_=stg).inc(SQ[pi], 16)
                        add_guard(("qkst", pi), SQ[pi])

                for oc in range(4):
                    prep_pump(2)
                    for sl in range(4):
                        guard(g, ("wst", sl))
                    g.dma_start(
                        out=wst[:, 0:8192].rearrange("p (dc o) -> p dc o", dc=DC),
                        in_=wqkv_h[:, 2 * D + oc * 512: 2 * D + (oc + 1) * 512].rearrange(
                            "(dc p) o -> p dc o", p=128)).inc("lw0", 16)
                    mk[("wv", b, oc)] = markv("lw0")
                    wmark(t, ("wv", b, oc))
                    for tt in range(DC):
                        pi = (oc * DC + tt) % 4
                        iq = oc * DC + tt
                        pguard(t, [pi])
                        for dc in range(DC):
                            t.matmul(PS4[pi],
                                     arena[:, AR_XFM + dc * 2048 + tt * 128:
                                           AR_XFM + dc * 2048 + (tt + 1) * 128],
                                     wst[:, dc * 512:(dc + 1) * 512],
                                     start=(dc == 0), stop=False)
                        t.matmul(PS4[pi], ones_r[0:1, 0:128],
                                 bqr[0:1, 2 * D + oc * 512: 2 * D + (oc + 1) * 512],
                                 start=False, stop=True).inc("tP", 1)
                        mk[("vmmc", b, iq)] = markv("tP")
                        if tt == DC - 1:
                            for sl in range(4):
                                add_guard(("wst", sl), "tP")
                        stg = arena[:, AR_QKST + pi * 512: AR_QKST + (pi + 1) * 512]
                        wmark(s, ("vmmc", b, iq))
                        guard(s, ("qkst", pi))
                        s.copy(stg, PS4[pi]).inc("aA", 1)
                        padd([pi], "aA")
                        mk[("vev", b, iq)] = markv("aA")
                        wmark(y, ("vev", b, iq))
                        y.dma_start(out=v_scr[b * T + tt * 128: b * T + (tt + 1) * 128,
                                              oc * 512:(oc + 1) * 512],
                                    in_=stg).inc(SQ[pi], 16)
                        add_guard(("qkst", pi), SQ[pi])
                add_guard(("arena_xfm",), "tP")
                mk[("qkvst", b)] = [markv(sq) for sq in SQ]

                guard(g, ("arena_xfm",))
                add_guard(("arena_xfm",), "tP")
                for dc in range(DC):
                    g.dma_start(out=arena[:, AR_XFM + dc * 2048: AR_XFM + (dc + 1) * 2048],
                                in_=wproj_h[dc * 128:(dc + 1) * 128, :]).inc("lwp", 16)
                mk[("wproj", b)] = markv("lwp")

                for h in range(H):
                    prep_pump(2)
                    hp = h % 2
                    guard(y, ("attld", hp))
                    wmark(y, ("qkvst", b))
                    y.dma_start(out=arena[:, AR_QH + hp * 2048: AR_QH + (hp + 1) * 2048],
                                in_=q_scr[(b * H + h) * 128:(b * H + h + 1) * 128, :]
                                ).inc(LT[hp], 16)
                    y.dma_start(out=arena[:, AR_KH + hp * 2048: AR_KH + (hp + 1) * 2048],
                                in_=k_scr[(b * H + h) * 128:(b * H + h + 1) * 128, :]
                                ).inc(LT[hp], 16)
                    y.dma_start(out=arena[:, AR_VH + hp * 2048: AR_VH + (hp + 1) * 2048
                                          ].rearrange("p (kt o) -> p kt o", kt=DC),
                                in_=v_scr[b * T:(b + 1) * T, h * 128:(h + 1) * 128].rearrange(
                                    "(kt p) o -> p kt o", p=128)).inc(LT[hp], 16)
                    mk[("attld", b, h)] = markv(LT[hp])
                    qh = arena[:, AR_QH + hp * 2048: AR_QH + (hp + 1) * 2048]
                    kh = arena[:, AR_KH + hp * 2048: AR_KH + (hp + 1) * 2048]
                    vh = arena[:, AR_VH + hp * 2048: AR_VH + (hp + 1) * 2048]
                    for half in range(2):
                        wmark(t, ("attld", b, h))
                        pguard(t, [4, 5])
                        pguard(t, [6, 7])
                        def emit_scores(kt):
                            Sp = PA if kt % 2 == 0 else PB
                            sbanks = [0, 1] if kt % 2 == 0 else [2, 3]
                            pguard(t, sbanks)
                            for qc in range(2):
                                t.matmul(Sp[:, qc * 512:(qc + 1) * 512],
                                         kh[:, kt * 128:(kt + 1) * 128],
                                         qh[:, half * 1024 + qc * 512:
                                            half * 1024 + (qc + 1) * 512],
                                         start=True, stop=True).inc("tP", 1)
                            mk[("score", b, h, half, kt)] = markv("tP")
                            pbt = arena[:, AR_PBT + (kt % 2) * 1024:
                                        AR_PBT + (kt % 2 + 1) * 1024]
                            wmark(s, ("score", b, h, half, kt))
                            guard(s, ("pbt", kt % 2))
                            s.activation(out=pbt, in_=Sp[:, 0:1024], func=AF.Exp,
                                         bias=c0col[:, 0:1], scale=SCALE).inc("aA", 1)
                            padd(sbanks, "aA")
                            mk[("exp", b, h, half, kt)] = markv("aA")

                        def emit_sumav(kt):
                            pbt = arena[:, AR_PBT + (kt % 2) * 1024:
                                        AR_PBT + (kt % 2 + 1) * 1024]
                            wmark(t, ("exp", b, h, half, kt))
                            for qc in range(2):
                                t.matmul(PD[0:1, qc * 512:(qc + 1) * 512], ones_c[:, 0:1],
                                         pbt[:, qc * 512:(qc + 1) * 512],
                                         start=(kt == 0), stop=(kt == DC - 1))
                            last = None
                            for qc in range(2):
                                last = t.matmul(PC[:, qc * 512:(qc + 1) * 512],
                                                vh[:, kt * 128:(kt + 1) * 128],
                                                pbt[:, qc * 512:(qc + 1) * 512],
                                                start=(kt == 0), stop=(kt == DC - 1))
                            last.inc("tP", 1)
                            add_guard(("pbt", kt % 2), "tP")
                            mk[("av", b, h, half, kt)] = markv("tP")

                        for kt in range(DC):
                            emit_scores(kt)
                            if kt >= 1:
                                emit_sumav(kt - 1)
                        emit_sumav(DC - 1)
                        if half == 1:
                            add_guard(("attld", hp), "tP")
                        mk[("avdone2", b, h, half)] = markv("tP")
                        wmark(s, ("avdone2", b, h, half))
                        guard(s, ("recipb",))
                        s.activation(out=recip[0:1, :], in_=PD[0:1, 0:1024],
                                     func=AF.Ln, scale=1.0).inc("aA", 1)
                        s.wait("aA", cnt["aA"])
                        s.activation(out=recipb[0:1, :], in_=recip[0:1, :],
                                     func=AF.Exp, scale=-1.0).inc("aA", 1)
                        padd([6, 7], "aA")
                        mk[("recipb", b, h, half)] = markv("aA")
                        wmark(t, ("recipb", b, h, half))
                        pguard(t, [0, 1])
                        last = None
                        for qc in range(2):
                            last = t.matmul(PA[:, qc * 512:(qc + 1) * 512],
                                            ones_r[0:1, 0:128],
                                            recipb[0:1, qc * 512:(qc + 1) * 512],
                                            start=True, stop=True)
                        last.inc("tP", 1)
                        add_guard(("recipb",), "tP")
                        mk[("rbc", b, h, half)] = markv("tP")
                        wmark(v, ("rbc", b, h, half))
                        v.tensor_copy(rcb[:, 0:1024], PA[:, 0:1024]).inc("vD", 1)
                        padd([0, 1], "vD")
                        v.wait("vD", cnt["vD"])
                        for qc in range(2):
                            stg = arena[:, AR_AVST + qc * 512: AR_AVST + (qc + 1) * 512]
                            guard(v, ("avst", qc))
                            v.tensor_tensor(out=stg, in0=PC[:, qc * 512:(qc + 1) * 512],
                                            in1=rcb[:, qc * 512:(qc + 1) * 512],
                                            op=ALU.mult).inc("vD", 1)
                            mk[("avev_c", b, h, half, qc)] = markv("vD")
                            if qc == 1:
                                padd([4, 5], "vD")
                            wmark(y, ("avev_c", b, h, half, qc))
                            y.dma_start(
                                out=ao_scr[(b * H + h) * 128:(b * H + h + 1) * 128,
                                           half * 1024 + qc * 512:
                                           half * 1024 + (qc + 1) * 512],
                                in_=stg).inc(SV[qc], 16)
                            add_guard(("avst", qc), SV[qc])
                mk[("ao_all", b)] = [markv(svn) for svn in SV]
                mk[("attn_done", b)] = markv("tP")

                wmark(g, ("attn_done", b))
                g.dma_start(out=bpr[0:1, :], in_=bproj_h[:, :]).inc("lbp", 16)
                mk[("bpr", b)] = markv("lbp")
                for tt in range(DC):
                    prep_pump(2)
                    par = tt % 2
                    guard(y, ("xst", par))
                    y.dma_start(out=xst[par][:, :],
                                in_=x_h[b * T + tt * 128: b * T + (tt + 1) * 128, :]
                                ).inc(LX[par], 16)
                    mk[("res", b, tt)] = markv(LX[par])
                    slot = tt % 4
                    guard(g, ("wst", slot))
                    wmark(g, ("ao_all", b))
                    g.dma_start(out=wst[:, slot * 2048:(slot + 1) * 2048].rearrange(
                                    "p (dc t) -> p dc t", dc=DC),
                                in_=ao_scr[b * DC * 128:(b + 1) * DC * 128,
                                           tt * 128:(tt + 1) * 128].rearrange(
                                    "(dc p) t -> p dc t", p=128)).inc(LW[slot], 16)
                    mk[("aold", b, tt)] = markv(LW[slot])
                    wmark(t, ("aold", b, tt))
                    wmark(t, ("wproj", b))
                    wmark(t, ("bpr", b))
                    for oc in range(4):
                        pi = (tt * 4 + oc) % 4
                        pguard(t, [pi])
                        for dc in range(DC):
                            t.matmul(PS4[pi],
                                     wst[:, slot * 2048 + dc * 128:
                                         slot * 2048 + (dc + 1) * 128],
                                     arena[:, AR_XFM + dc * 2048 + oc * 512:
                                           AR_XFM + dc * 2048 + (oc + 1) * 512],
                                     start=(dc == 0), stop=False)
                        t.matmul(PS4[pi], ones_r[0:1, 0:128],
                                 bpr[0:1, oc * 512:(oc + 1) * 512],
                                 start=False, stop=True).inc("tP", 1)
                        mk[("projc", b, tt, oc)] = markv("tP")
                        if oc == 3:
                            add_guard(("wst", slot), "tP")
                            add_guard(("arena_xfm",), "tP")
                        wmark(v, ("projc", b, tt, oc))
                        wmark(v, ("res", b, tt))
                        if oc == 0:
                            guard(v, ("pst", par))
                        v.tensor_tensor(out=pst[par][:, oc * 512:(oc + 1) * 512],
                                        in0=PS4[pi],
                                        in1=xst[par][:, oc * 512:(oc + 1) * 512],
                                        op=ALU.add).inc("vD", 1)
                        if oc == 3:
                            add_guard(("xst", par), "vD")
                        padd([pi], "vD")
                        mk[("projadd", b, tt, oc)] = markv("vD")
                    guard(v, ("cst", par))
                    v.wait("vD", cnt["vD"])
                    v.tensor_scalar(out=cst[par][:, :], in0=pst[par][:, :],
                                    scalar1=RD, scalar2=None, op0=ALU.mult,
                                    op1=ALU.add, accum_out=mean[:, 0:1]).inc("vD", 1)
                    mk[("lnmean", b, tt)] = markv("vD")
                    v.wait("vD", cnt["vD"])
                    wmark(s, ("lnmean", b, tt))
                    guard(s, ("trst",))
                    s.activation(out=arena[:, AR_TRST:AR_TRST + 2048],
                                 in_=pst[par][:, :], func=AF.Square, scale=RSD,
                                 accum_out=ssq[:, 0:1]).inc("aA", 1)
                    mk[("lnssq", b, tt)] = markv("aA")
                    wmark(v, ("lnssq", b, tt))
                    v.scalar_tensor_tensor(out=var[:, 0:1], in0=mean[:, 0:1],
                                           scalar=mean[:, 0:1], in1=ssq[:, 0:1],
                                           op0=ALU.mult, op1=ALU.subtract).inc("vD", 1)
                    mk[("lnvar", b, tt)] = markv("vD")
                    wmark(s, ("lnvar", b, tt))
                    s.activation(out=std[:, 0:1], in_=var[:, 0:1], func=AF.Sqrt,
                                 bias=epscol[:, 0:1], scale=-1.0).inc("aA", 1)
                    mk[("lnstd", b, tt)] = markv("aA")
                    wmark(v, ("lnstd", b, tt))
                    v.reciprocal(rstd[:, 0:1], std[:, 0:1]).inc("vD", 1)
                    v.wait("vD", cnt["vD"])
                    v.tensor_scalar(out=pst[par][:, :], in0=pst[par][:, :],
                                    scalar1=mean[:, 0:1], scalar2=rstd[:, 0:1],
                                    op0=ALU.subtract, op1=ALU.mult).inc("vD", 1)
                    mk[("x1", b, tt)] = markv("vD")
                    wmark(y, ("x1", b, tt))
                    y.dma_start(out=x1tm_scr[b * T + tt * 128: b * T + (tt + 1) * 128, :],
                                in_=pst[par][:, :]).inc(SM[par], 16)
                    add_guard(("pst", par), SM[par])
                    wmark(s, ("x1", b, tt))
                    guard(s, ("cst", par))
                    s.copy(cst[par][:, :], pst[par][:, :]).inc("aA", 1)
                    mk[("x1bf", b, tt)] = markv("aA")
                    guard(s, ("trst",))
                    for dc in range(DC):
                        s.dma_start_transpose(
                            out=arena[:, AR_TRST + dc * 128: AR_TRST + (dc + 1) * 128],
                            in_=cst[par][:, dc * 128:(dc + 1) * 128]).inc("sxt", 16)
                    add_guard(("cst", par), "sxt")
                    mk[("x1trd", b, tt)] = markv("sxt")
                    wmark(s, ("x1trd", b, tt))
                    s.dma_start(out=x1fm_scr[b * DC * 128:(b + 1) * DC * 128,
                                             tt * 128:(tt + 1) * 128].rearrange(
                                    "(dc p) t -> p dc t", p=128),
                                in_=arena[:, AR_TRST:AR_TRST + 2048].rearrange(
                                    "p (dc t) -> p dc t", dc=DC)).inc("sxf", 16)
                    add_guard(("trst",), "sxf")
                mk[("x1_all", b)] = markv("sxf")
                mk[("x1tm_all", b)] = [markv(smn) for smn in SM]
                mk[("projmm_done", b)] = markv("tP")

            wmark(g, ("prepall",))
            for src_t, dst_t, mkey in ((wg_src, wg_scr, ("ccwg",)),
                                       (wu_src, wu_scr, ("ccwu",)),
                                       (wd_src, wd_scr, ("ccwd",))):
                g.collective_compute(
                    "AllGather", ALU.bypass,
                    replica_groups=[list(range(8))],
                    ins=[src_t.ap().opt()], outs=[dst_t.ap().opt()]).inc("ccs", 1)
                mk[mkey] = markv("ccs")
            for b in range(NB):
                for ch in range(2):
                    cidx = b * 2 + ch
                    guard(y, ("ar_x1fm",))
                    wmark(y, ("x1_all", NB - 1))
                    y.dma_start(out=arena[:, AR_X1FM:AR_X1FM + 16384].rearrange(
                                    "p (dc t) -> p dc t", dc=DC),
                                in_=x1fm_scr[b * DC * 128:(b + 1) * DC * 128,
                                             ch * 1024:(ch + 1) * 1024].rearrange(
                                    "(dc p) t -> p dc t", p=128)).inc("lxf", 16)
                    mk[("x1fmld", cidx)] = markv("lxf")

                    for it in range(NIT):
                        if pstate["comp"] < n_prep:
                            prep_pump(1)
                        wgslot = it % 2
                        wuslot = 2 + it % 2
                        guard(g, ("wst", wgslot))
                        wmark(g, ("ccwg",))
                        g.dma_start(out=wst[:, wgslot * 2048:(wgslot + 1) * 2048].rearrange(
                                        "p (hc i) -> p hc i", hc=DC),
                                    in_=wg_scr[:, it * 128:(it + 1) * 128].rearrange(
                                        "(hc p) i -> p hc i", p=128)).inc(LW[wgslot], 16)
                        mk[("wgld", cidx, it)] = markv(LW[wgslot])
                        guard(g, ("wst", wuslot))
                        wmark(g, ("ccwu",))
                        g.dma_start(out=wst[:, wuslot * 2048:(wuslot + 1) * 2048].rearrange(
                                        "p (hc i) -> p hc i", hc=DC),
                                    in_=wu_scr[:, it * 128:(it + 1) * 128].rearrange(
                                        "(hc p) i -> p hc i", p=128)).inc(LW[wuslot], 16)
                        mk[("wuld", cidx, it)] = markv(LW[wuslot])
                        Gp, gbanks = (PA, [0, 1]) if it % 2 == 0 else (PC, [4, 5])
                        Up, ubanks = (PB, [2, 3]) if it % 2 == 0 else (PD, [6, 7])
                        wmark(t, ("wgld", cidx, it))
                        wmark(t, ("x1fmld", cidx))
                        pguard(t, gbanks)
                        last = None
                        for tc2 in range(2):
                            for hc in range(DC):
                                last = t.matmul(
                                    Gp[:, tc2 * 512:(tc2 + 1) * 512],
                                    wst[:, wgslot * 2048 + hc * 128:
                                        wgslot * 2048 + (hc + 1) * 128],
                                    arena[:, AR_X1FM + hc * 1024 + tc2 * 512:
                                          AR_X1FM + hc * 1024 + (tc2 + 1) * 512],
                                    start=(hc == 0), stop=(hc == DC - 1))
                        last.inc("tP", 1)
                        add_guard(("wst", wgslot), "tP")
                        mk[("gmm", cidx, it)] = markv("tP")
                        wmark(t, ("wuld", cidx, it))
                        pguard(t, ubanks)
                        for tc2 in range(2):
                            for hc in range(DC):
                                last = t.matmul(
                                    Up[:, tc2 * 512:(tc2 + 1) * 512],
                                    wst[:, wuslot * 2048 + hc * 128:
                                        wuslot * 2048 + (hc + 1) * 128],
                                    arena[:, AR_X1FM + hc * 1024 + tc2 * 512:
                                          AR_X1FM + hc * 1024 + (tc2 + 1) * 512],
                                    start=(hc == 0), stop=(hc == DC - 1))
                        last.inc("tP", 1)
                        add_guard(("wst", wuslot), "tP")
                        if it == NIT - 1:
                            add_guard(("ar_x1fm",), "tP")
                        mk[("umm", cidx, it)] = markv("tP")
                        gbs = arena[:, AR_GB + (it % 2) * 1024: AR_GB + (it % 2 + 1) * 1024]
                        wmark(s, ("gmm", cidx, it))
                        guard(s, ("gb", it % 2))
                        s.activation(out=gbs, in_=Gp[:, 0:1024], func=AF.Silu,
                                     bias=0.0, scale=1.0).inc("aA", 1)
                        padd(gbanks, "aA")
                        mk[("silu", cidx, it)] = markv("aA")
                        wmark(v, ("umm", cidx, it))
                        wmark(v, ("silu", cidx, it))
                        if it == 0:
                            guard(v, ("ar_h",))
                        v.tensor_tensor(out=arena[:, AR_H + it * 1024: AR_H + (it + 1) * 1024],
                                        in0=Up[:, 0:1024], in1=gbs, op=ALU.mult).inc("vD", 1)
                        padd(ubanks, "vD")
                        add_guard(("gb", it % 2), "vD")
                        mk[("hmul", cidx, it)] = markv("vD")

                    for oc in range(4):
                        for ic in range(NIT):
                            wp = ic % 2
                            guard(g, ("wdst", wp))
                            if cidx == 0 and oc == 0:
                                wmark(g, ("ccwd",))
                            g.dma_start(out=wdst[wp][:, :],
                                        in_=wd_scr[ic * 128:(ic + 1) * 128,
                                                   oc * 512:(oc + 1) * 512]).inc(LD_[wp], 16)
                            mk[("wdld", cidx, oc, ic)] = markv(LD_[wp])
                            wmark(t, ("wdld", cidx, oc, ic))
                            if ic == 0:
                                wmark(t, ("hmul", cidx, NIT - 1))
                                for tt2 in range(8):
                                    pguard(t, [tt2])
                            last = None
                            for tt2 in range(8):
                                last = t.matmul(
                                    PS8[tt2],
                                    arena[:, AR_H + ic * 1024 + tt2 * 128:
                                          AR_H + ic * 1024 + (tt2 + 1) * 128],
                                    wdst[wp][:, :],
                                    start=(ic == 0), stop=(ic == NIT - 1))
                            last.inc("tP", 1)
                            add_guard(("wdst", wp), "tP")
                        mk[("dmmoc", cidx, oc)] = markv("tP")
                        if oc == 3:
                            add_guard(("ar_h",), "tP")
                        for tt2 in range(8):
                            gtile = b * T + ch * 1024 + tt2 * 128
                            par = tt2 % 2
                            guard(y, ("xst", par))
                            wmark(y, ("x1tm_all", b))
                            y.dma_start(out=xst[par][:, oc * 512:(oc + 1) * 512],
                                        in_=x1tm_scr[gtile:gtile + 128,
                                                     oc * 512:(oc + 1) * 512]
                                        ).inc(LX[par], 16)
                            mk[("x1res", cidx, oc, tt2)] = markv(LX[par])
                            si = par * 4 + ((tt2 // 2) % 2) * 2 + oc % 2
                            stg = pst[par][:, ((tt2 // 2) % 2) * 1024 + (oc % 2) * 512:
                                           ((tt2 // 2) % 2) * 1024 + (oc % 2 + 1) * 512]
                            wmark(v, ("dmmoc", cidx, oc))
                            wmark(v, ("x1res", cidx, oc, tt2))
                            if cidx == 0 and oc == 0:
                                wmark(v, ("x1tm_all", NB - 1))
                            guard(v, ("stg", si))
                            guard(v, ("cst", par))
                            v.tensor_tensor(out=stg, in0=PS8[tt2],
                                            in1=xst[par][:, oc * 512:(oc + 1) * 512],
                                            op=ALU.add).inc("vD", 1)
                            v.wait("vD", cnt["vD"])
                            v.tensor_scalar(out=cst[par][:, 0:512], in0=stg,
                                            scalar1=RD, scalar2=None, op0=ALU.mult,
                                            op1=ALU.add,
                                            accum_out=pmean[:, tt2 * 4 + oc:
                                                            tt2 * 4 + oc + 1]).inc("vD", 1)
                            padd([tt2], "vD")
                            add_guard(("xst", par), "vD")
                            mk[("dadd", cidx, oc, tt2)] = markv("vD")
                            wmark(s, ("dadd", cidx, oc, tt2))
                            s.activation(out=cst[par][:, 512:1024], in_=stg,
                                         func=AF.Square, scale=RSD,
                                         accum_out=pssq[:, tt2 * 4 + oc:
                                                        tt2 * 4 + oc + 1]).inc("aA", 1)
                            mk[("dsq", cidx, oc, tt2)] = markv("aA")
                            s.dma_start(out=x2_scr[gtile:gtile + 128,
                                                   oc * 512:(oc + 1) * 512],
                                        in_=stg).inc(SG[si], 16)
                            add_guard(("stg", si), SG[si])
                            mk[("x2st", cidx, oc, tt2)] = markv(SG[si])

                    for tt2 in range(8):
                        gtile = b * T + ch * 1024 + tt2 * 128
                        par = tt2 % 2
                        guard(y, ("xst", par))
                        for oc in range(4):
                            wmark(y, ("x2st", cidx, oc, tt2))
                        y.dma_start(out=xst[par][:, :],
                                    in_=x2_scr[gtile:gtile + 128, :]).inc(LX[par], 16)
                        mk[("x2ld", cidx, tt2)] = markv(LX[par])
                        wmark(v, ("x2ld", cidx, tt2))
                        wmark(v, ("dsq", cidx, 3, tt2))
                        v.wait("vD", cnt["vD"])
                        v.tensor_reduce(mean[:, 0:1], pmean[:, tt2 * 4:(tt2 + 1) * 4],
                                        axis=mybir.AxisListType.X, op=ALU.add)
                        v.tensor_reduce(ssq[:, 0:1], pssq[:, tt2 * 4:(tt2 + 1) * 4],
                                        axis=mybir.AxisListType.X, op=ALU.add).inc("vD", 1)
                        v.wait("vD", cnt["vD"])
                        v.scalar_tensor_tensor(out=var[:, 0:1], in0=mean[:, 0:1],
                                               scalar=mean[:, 0:1], in1=ssq[:, 0:1],
                                               op0=ALU.mult, op1=ALU.subtract).inc("vD", 1)
                        mk[("fvar", cidx, tt2)] = markv("vD")
                        wmark(s, ("fvar", cidx, tt2))
                        s.activation(out=std[:, 0:1], in_=var[:, 0:1], func=AF.Sqrt,
                                     bias=epscol[:, 0:1], scale=-1.0).inc("aA", 1)
                        mk[("fstd", cidx, tt2)] = markv("aA")
                        wmark(v, ("fstd", cidx, tt2))
                        v.reciprocal(rstd[:, 0:1], std[:, 0:1]).inc("vD", 1)
                        v.wait("vD", cnt["vD"])
                        v.tensor_scalar(out=xst[par][:, :], in0=xst[par][:, :],
                                        scalar1=mean[:, 0:1], scalar2=rstd[:, 0:1],
                                        op0=ALU.subtract, op1=ALU.mult).inc("vD", 1)
                        mk[("outn", cidx, tt2)] = markv("vD")
                        wmark(y, ("outn", cidx, tt2))
                        y.dma_start(out=out_h[gtile:gtile + 128, :],
                                    in_=xst[par][:, :]).inc(SO[par], 16)
                        add_guard(("xst", par), SO[par])

            assert pstate["comp"] == n_prep, (pstate["comp"], n_prep)
            for semn in DMA_SEMS:
                y.wait(semn, cnt[semn])

        @block.tensor
        def _(eng):
            walk(_Walk(semh, "t", eng))

        @block.vector
        def _(eng):
            walk(_Walk(semh, "v", eng))

        @block.scalar
        def _(eng):
            walk(_Walk(semh, "s", eng))

        @block.gpsimd
        def _(eng):
            walk(_Walk(semh, "g", eng))

        @block.sync
        def _(eng):
            walk(_Walk(semh, "y", eng))

    return nc


def kernel(L0_attn_x, L0_attn_w_qkv, L0_attn_b_qkv, L0_attn_w_proj, L0_attn_b_proj,
           L0_ffn_lut, L0_ffn_gate_idx, L0_ffn_up_idx, L0_ffn_down_idx):
    x = np.ascontiguousarray(np.asarray(L0_attn_x, dtype=np.float32))
    w_qkv = np.ascontiguousarray(np.asarray(L0_attn_w_qkv, dtype=np.float32))
    b_qkv = np.ascontiguousarray(np.asarray(L0_attn_b_qkv, dtype=np.float32).reshape(1, 3 * D))
    w_proj = np.ascontiguousarray(np.asarray(L0_attn_w_proj, dtype=np.float32))
    b_proj = np.ascontiguousarray(np.asarray(L0_attn_b_proj, dtype=np.float32).reshape(1, D))
    lut = np.ascontiguousarray(np.asarray(L0_ffn_lut, dtype=np.float32).reshape(1, 16))
    gidxT = np.ascontiguousarray(np.asarray(L0_ffn_gate_idx, dtype=np.int32).T)
    uidxT = np.ascontiguousarray(np.asarray(L0_ffn_up_idx, dtype=np.int32).T)
    didxT = np.ascontiguousarray(np.asarray(L0_ffn_down_idx, dtype=np.int32).T)
    ident = np.eye(128, dtype=np.float32)

    nc = _build_nc()
    in_maps = []
    rg = D // 8
    rd = ITER // 8
    for c in range(8):
        xs = np.ascontiguousarray(x[2 * c:2 * c + 2].reshape(NB * T, D))
        in_maps.append({
            "x": xs, "w_qkv": w_qkv, "b_qkv": b_qkv, "w_proj": w_proj,
            "b_proj": b_proj, "lut": lut,
            "gidxT": np.ascontiguousarray(gidxT[c * rg:(c + 1) * rg]),
            "uidxT": np.ascontiguousarray(uidxT[c * rg:(c + 1) * rg]),
            "didxT": np.ascontiguousarray(didxT[c * rd:(c + 1) * rd]),
            "ident": ident,
        })
    res = run_bass_kernel_spmd(nc, in_maps, core_ids=list(range(8)))
    out = np.concatenate([r["out"].reshape(NB, T, D) for r in res.results], axis=0)
    return out
